# version 38
# baseline (speedup 1.0000x reference)
import contextlib
import numpy as np
import concourse.bass as bass
import concourse.mybir as mybir
from concourse.bass_utils import run_bass_kernel_spmd

F32 = mybir.dt.float32
BF16 = mybir.dt.bfloat16
U8 = mybir.dt.uint8
AF = mybir.ActivationFunctionType
ALU = mybir.AluOpType
AX = mybir.AxisListType

P = 128
B_LOC = 2
S = 2048
D = 1024
KT = 8
NT = 16
MAGIC = float(1.5 * 2.0 ** 23)

SEMS = ("sQ", "sW", "sG", "sT", "sV", "sS", "sP", "sO")


class Em:

    def __init__(self, nc, stack):
        self.nc = nc
        self.plan = {e: [] for e in ("sync", "vector", "scalar", "tensor", "gpsimd")}
        self.cnt = {s: 0 for s in SEMS}
        self.sem = {s: stack.enter_context(nc.semaphore(s)) for s in SEMS}

    def op(self, eng, method, *args, inc=None, amt=1, **kw):
        if inc is not None:
            self.cnt[inc] += amt
        sem = self.sem

        def fn(e, m=method, a=args, k=kw, i=inc, n=amt):
            r = getattr(e, m)(*a, **k)
            if i is not None:
                r.then_inc(sem[i], n)
        self.plan[eng].append(fn)
        return self.cnt[inc] if inc is not None else None

    def wait(self, eng, s, val):
        assert val is not None and val <= self.cnt[s], (eng, s, val, self.cnt[s])
        sem = self.sem
        self.plan[eng].append(lambda e, ss=s, v=val: e.wait_ge(sem[ss], v))

    def v(self, s):
        return self.cnt[s]


def build_nc(mode="lmax8", dbg=False):
    nc = bass.Bass("TRN2", target_bir_lowering=False)

    X = nc.dram_tensor("X", [B_LOC, S, D], F32, kind="ExternalInput")
    Wdr = {("z", 0): nc.dram_tensor("WZ_f", [D, D], F32, kind="ExternalInput"),
           ("z", 1): nc.dram_tensor("WZ_b", [D, D], F32, kind="ExternalInput"),
           ("h", 0): nc.dram_tensor("WH_f", [D, D], F32, kind="ExternalInput"),
           ("h", 1): nc.dram_tensor("WH_b", [D, D], F32, kind="ExternalInput")}
    BZf = nc.dram_tensor("BZ_f", [D], F32, kind="ExternalInput")
    BZb = nc.dram_tensor("BZ_b", [D], F32, kind="ExternalInput")
    BHf = nc.dram_tensor("BH_f", [D], F32, kind="ExternalInput")
    BHb = nc.dram_tensor("BH_b", [D], F32, kind="ExternalInput")
    WOt = nc.dram_tensor("WO", [2 * D, D], F32, kind="ExternalInput")
    BOt = nc.dram_tensor("BO", [D], F32, kind="ExternalInput")
    FWt = nc.dram_tensor("FW", [D, D], F32, kind="ExternalInput")
    FBt = nc.dram_tensor("FB", [D], F32, kind="ExternalInput")
    EYEt = nc.dram_tensor("EYE", [P, P], F32, kind="ExternalInput")
    OUT = nc.dram_tensor("OUT", [B_LOC, S, D], F32, kind="ExternalOutput")
    if mode == "cc":
        CCI = nc.dram_tensor("cc_in", [1], F32)
        CCO = nc.dram_tensor("cc_out", [1], F32)
    else:
        LM8 = nc.dram_tensor("LMAX8", [8], F32, kind="ExternalInput")

    st = contextlib.ExitStack()
    with st:
        sb = lambda n, sh, dt: st.enter_context(nc.sbuf_tensor(n, sh, dt))

        xn_bfT = sb("xn_bfT", [P, KT * S], BF16)
        q_bfT = sb("q_bfT", [P, KT * S], BF16)
        qwT = sb("qwT", [P, KT * D], BF16)
        wz_blk = sb("wz_blk", [P, 3 * KT * P], BF16)
        wh_blk = sb("wh_blk", [P, 3 * KT * P], BF16)
        wo_blk = sb("wo_blk", [P, 3 * 16 * P], BF16)
        g1 = sb("g1", [P, 32 * 1024], U8)
        g2 = sb("g2", [P, 48 * 1024], U8)
        g3 = sb("g3", [P, 20 * 1024], U8)
        msum = sb("msum", [P, B_LOC * NT], F32)
        sqsum = sb("sqsum", [P, B_LOC * NT], F32)
        mean2 = sb("mean2", [P, B_LOC * NT], F32)
        varv = sb("varv", [P, B_LOC * NT], F32)
        negmu = sb("negmu", [P, B_LOC * NT], F32)
        inv_a = sb("inv_a", [P, B_LOC * NT], F32)
        invr_a = sb("invr_a", [P, B_LOC * NT], F32)
        lmax = sb("lmax", [P, 1], F32)
        mx_a = sb("mx_a", [P, B_LOC * NT], F32)
        mn_a = sb("mn_a", [P, B_LOC * NT], F32)
        sd_a = sb("sd_a", [P, B_LOC * NT], F32)
        asum = sb("asum", [P, 1], F32)
        grow = sb("grow", [1, P], F32)
        gmax = sb("gmax", [1, 1], F32)
        sc11 = sb("sc11", [1, 8], F32)
        ones_r = sb("ones_r", [1, P], F32)
        rb = sb("rb", [P, 2], F32)
        thb = sb("thb", [P, 1], F32)
        negth = sb("negth", [P, 1], F32)
        biases = sb("biases", [P, 6 * KT], F32)
        negbz = sb("negbz", [P, 2 * KT], F32)
        bh05 = sb("bh05", [P, 2 * KT], F32)
        carry = sb("carry", [P, 16], F32)
        eye32 = sb("eye32", [P, P], F32)
        eps_b = sb("eps_b", [P, 1], F32)
        c127 = sb("c127", [1, 1], F32)
        grow_lm = sb("grow_lm", [1, 8], F32)

        bc = lambda ap, dt: ap.bitcast(dt)
        fw32 = bc(g1.ap()[:, 0:32768], F32)
        h_bf = bc(g1.ap()[:, 0:32768], BF16)
        sign_s = bc(g2.ap()[:, 0:16384], BF16)
        mask_s = bc(g2.ap()[:, 16384:32768], BF16)
        mask2_s = bc(g2.ap()[:, 32768:49152], BF16)
        a_arr = [bc(g2.ap()[:, i * 4096:(i + 1) * 4096], F32) for i in (0, 1)]
        b_arr = [bc(g2.ap()[:, i * 4096:(i + 1) * 4096], F32) for i in (2, 3)]
        z_bufs = [bc(g2.ap()[:, 16384 + i * 2048: 16384 + (i + 1) * 2048], F32) for i in (0, 1)]
        sv_bufs = [bc(g2.ap()[:, 20480 + i * 2048: 20480 + (i + 1) * 2048], F32) for i in (0, 1)]
        t_buf = bc(g2.ap()[:, 24576:26624], F32)
        gh_buf = bc(g2.ap()[:, 26624:28672], F32)
        p_half = bc(g2.ap()[:, 0:32768], F32)
        s_m = bc(g2.ap()[:, 32768:36864], F32)
        ffn_m = bc(g2.ap()[:, 36864:40960], F32)
        sig_m = bc(g2.ap()[:, 40960:45056], F32)
        gg_m = bc(g2.ap()[:, 45056:49152], F32)
        x_in4 = [bc(g3.ap()[:, i * 4096:(i + 1) * 4096], F32) for i in range(3)]
        qw_f = bc(g3.ap()[:, 12288:16384], F32)
        xn32_st = bc(g3.ap()[:, 12288:16384], F32)
        q32_st = bc(g3.ap()[:, 16384:20480], F32)
        x_epi = [x_in4[0], x_in4[1]]
        out_sb = [bc(g3.ap()[:, 12288:16384], F32), bc(g3.ap()[:, 16384:20480], F32)]

        pk = [st.enter_context(nc.psum_tensor(f"pk{i}", [P, 512], F32)) for i in (0, 1)]
        pv = [st.enter_context(nc.psum_tensor(f"pv{i}", [P, 512], F32)) for i in (0, 1)]
        pt = [st.enter_context(nc.psum_tensor(f"pt{i}", [P, 1024], F32)) for i in (0, 1)]

        em = Em(nc, st)
        E, W = em.op, em.wait

        E("sync", "dma_start", inc="sQ", amt=16, out=eye32[:], in_=EYEt[:])
        for j, bt in enumerate((BZf, BZb, BHf, BHb, BOt, FBt)):
            E("sync", "dma_start", inc="sQ", amt=16,
              out=biases[:, j * KT:(j + 1) * KT],
              in_=bt.ap().rearrange("(k p) -> p k", p=P),
              allow_slow_non_contiguous=True)
        E("sync", "dma_start", inc="sQ", amt=16,
          out=fw32.rearrange("p (k n) -> p k n", k=KT),
          in_=FWt.ap().rearrange("(k p) n -> p k n", p=P))
        ld_const = em.v("sQ")

        E("vector", "memset", ones_r[:1, :], 1.0)
        E("vector", "memset", eps_b[:], 1e-5)
        E("vector", "memset", c127[:1, :], 127.0)
        E("vector", "memset", sc11[:1, :], 1.0, inc="sV")

        W("scalar", "sQ", ld_const)
        E("scalar", "activation", negbz[:], biases[:, 0:2 * KT], AF.Copy, scale=-1.0)
        E("scalar", "activation", sign_s, fw32, AF.Sign, inc="sS")
        s_sign = em.v("sS")
        W("vector", "sQ", ld_const)
        E("vector", "tensor_scalar_add", bh05[:], biases[:, 2 * KT:4 * KT], 0.5, inc="sV")

        W("vector", "sQ", ld_const)
        E("vector", "tensor_reduce", asum[:, 0:1], fw32, AX.X, ALU.add,
          apply_absolute_value=True, inc="sV")
        v_asum = em.v("sV")
        W("gpsimd", "sV", v_asum)
        E("gpsimd", "dma_start", inc="sG", amt=16, out=grow[:1, :], in_=asum[:, 0:1])
        g_asum = em.v("sG")
        W("vector", "sG", g_asum)
        E("vector", "tensor_reduce", sc11[:1, 6:7], grow[:1, :], AX.X, ALU.add,
          inc="sV")
        v_wsum = em.v("sV")
        W("scalar", "sV", v_wsum)
        E("scalar", "activation", sc11[:1, 3:4], sc11[:1, 6:7], AF.Copy,
          scale=float(2.0 ** -20))
        E("scalar", "activation", sc11[:1, 4:5], sc11[:1, 6:7], AF.Copy,
          scale=float(2.0 ** -21), inc="sS")
        s_wsth = em.v("sS")
        W("tensor", "sS", s_wsth)
        E("tensor", "matmul", pt[0][:, 0:1], ones_r[:1, :], sc11[:1, 4:5],
          start=True, stop=True, inc="sT")
        t_th = em.v("sT")
        W("scalar", "sT", t_th)
        E("scalar", "copy", thb[:, 0:1], pt[0][:, 0:1])
        E("scalar", "activation", negth[:, 0:1], pt[0][:, 0:1], AF.Copy,
          scale=-1.0, inc="sS")
        s_th = em.v("sS")
        W("vector", "sS", s_th)
        E("vector", "tensor_scalar", mask_s, fw32, thb[:, 0:1], 0.0,
          ALU.is_gt, ALU.add)
        E("vector", "tensor_scalar", mask2_s, fw32, negth[:, 0:1], 0.0,
          ALU.is_lt, ALU.add)
        E("vector", "tensor_tensor", mask_s, mask_s, mask2_s, ALU.add, inc="sV")
        v_mask = em.v("sV")

        W("gpsimd", "sS", s_sign)
        W("gpsimd", "sV", v_mask)
        qw_tp = []
        qw_ev = []
        for m in range(KT):
            if m >= 1:
                W("gpsimd", "sT", qw_tp[m - 1])
            E("gpsimd", "tensor_tensor", qw_f[:],
              sign_s[:, m * D:(m + 1) * D], mask_s[:, m * D:(m + 1) * D],
              ALU.mult, inc="sP")
            p_qw = em.v("sP")
            W("tensor", "sP", p_qw)
            if m >= 2:
                W("tensor", "sS", qw_ev[m - 2])
            for k in range(KT):
                E("tensor", "transpose", pt[m % 2][:, k * 128:(k + 1) * 128],
                  qw_f[:, k * 128:(k + 1) * 128], eye32[:],
                  inc=("sT" if k == KT - 1 else None))
            qw_tp.append(em.v("sT"))
            W("scalar", "sT", qw_tp[-1])
            E("scalar", "copy",
              qwT.ap().rearrange("p (k n) -> p k n", k=KT)[:, :, m * 128:(m + 1) * 128],
              pt[m % 2].ap().rearrange("p (k n) -> p k n", k=KT),
              inc="sS")
            qw_ev.append(em.v("sS"))
        s_qwT = em.v("sS")
        t_qw_last = qw_tp[-1]

        acc_dummy = bc(g2.ap()[:, 32768:36864], F32)
        s_inv = 0
        rc = {}

        def emit_rchain():
            v_gmax2 = em.v("sV")
            W("scalar", "sV", v_gmax2)
            if mode == "cc":
                W("scalar", "sG", em.v("sG"))
            E("scalar", "copy", sc11[:1, 7:8], gmax[:1, 0:1], inc="sS")
            s_cp1 = em.v("sS")
            W("vector", "sS", s_cp1)
            E("vector", "tensor_scalar_mul", sc11[:1, 0:1], sc11[:1, 7:8],
              float(np.float32(1.0) / np.float32(127.0)))
            E("vector", "reciprocal", sc11[:1, 6:7], sc11[:1, 7:8], inc="sV")
            v_rg = em.v("sV")
            W("gpsimd", "sV", v_rg)
            E("gpsimd", "tensor_tensor", sc11[:1, 1:2], sc11[:1, 6:7],
              c127[:1, 0:1], ALU.mult)
            E("gpsimd", "tensor_tensor", sc11[:1, 2:3], sc11[:1, 3:4],
              sc11[:1, 0:1], ALU.mult, inc="sP")
            p_wsis = em.v("sP")
            W("tensor", "sP", p_wsis)
            W("tensor", "sS", s_qwT)
            E("tensor", "matmul", pt[0][:, 0:2], ones_r[:1, :], sc11[:1, 1:3],
              start=True, stop=True, inc="sT")
            t_rb = em.v("sT")
            W("scalar", "sT", t_rb)
            E("scalar", "copy", rb[:], pt[0][:, 0:2], inc="sS")
            rc["s_rb"] = em.v("sS")

        if mode != "cc":
            E("sync", "dma_start", inc="sQ", amt=16, out=grow_lm[:1, 0:8],
              in_=LM8.ap()[None, :])
            g_lm8 = em.v("sQ")
            W("vector", "sQ", g_lm8)
            E("vector", "tensor_reduce", gmax[:1, 0:1], grow_lm[:1, 0:8], AX.X,
              ALU.max, inc="sV")
            emit_rchain()

        def emit_localmax(dst):
            E("vector", "tensor_tensor", varv[:], mx_a[:], negmu[:], ALU.add)
            E("vector", "tensor_tensor", mx_a[:], mn_a[:], negmu[:], ALU.add,
              inc="sV")
            W("gpsimd", "sV", em.v("sV"))
            E("gpsimd", "tensor_tensor", mn_a[:], varv[:], inv_a[:], ALU.mult)
            E("gpsimd", "tensor_tensor", varv[:], mx_a[:], inv_a[:], ALU.mult,
              inc="sP")
            W("vector", "sP", em.v("sP"))
            E("vector", "scalar_tensor_tensor", mx_a[:], varv[:], -1.0, mn_a[:],
              ALU.mult, ALU.max, inc="sV")
            W("scalar", "sV", em.v("sV"))
            E("scalar", "copy", sd_a[:], mx_a[:], inc="sS")
            W("vector", "sS", em.v("sS"))
            E("vector", "tensor_reduce", lmax[:, 0:1], sd_a[:], AX.X, ALU.max,
              inc="sV")
            W("gpsimd", "sV", em.v("sV"))
            E("gpsimd", "dma_start", inc="sG", amt=16, out=grow[:1, :],
              in_=lmax[:, 0:1])
            W("vector", "sG", em.v("sG"))
            E("vector", "tensor_reduce", dst, grow[:1, :], AX.X,
              ALU.max, inc="sV")

        def emit_gmax_cc():
            emit_localmax(gmax[:1, 0:1])
            W("gpsimd", "sV", em.v("sV"))
            E("gpsimd", "dma_start", inc="sG", amt=16, out=CCI.ap()[0:1],
              in_=gmax[:1, 0:1])
            W("gpsimd", "sG", em.v("sG"))
            E("gpsimd", "collective_compute", "AllReduce", ALU.max,
              replica_groups=[list(range(8))], ins=[CCI.ap()[0:1]],
              outs=[CCO.ap()[0:1]], inc="sG", amt=16)
            W("gpsimd", "sG", em.v("sG"))
            E("gpsimd", "dma_start", inc="sG", amt=16, out=gmax[:1, 0:1],
              in_=CCO.ap()[None, 0:1])
            emit_rchain()

        st_g = {
            "n_out_dma": 0,
            "prev_epi": {"sV": 0, "sO": 0, "sT": 0, "sS": 0, "sP": 0},
            "out_add_v": {0: 0, 1: 0},
            "out_dma_o": {0: 0, 1: 0},
            "wslot_hist": {0: 0, 1: 0, 2: 0},
            "woslot_hist": {0: 0, 1: 0, 2: 0},
            "pre_issued": {},
            "epi_smm_done": 0,
            "stream_tail": {"sS": 0, "sT": 0, "sV": 0},
            "p0_done": {},
            "q_done": {},
            "acc_done": {},
            "carry_cnt": {},
            "all_scans": {},
            "ep_p_written": 0,
        }
        seq = [(d, ptl) for d in range(2) for ptl in range(KT)]
        acc_dummy_w = bc(wz_blk.ap()[:, 0:2048], F32)
        acc_dummy_s = bc(g2.ap()[:, 32768:36864], F32)

        def stream_batch(bi, do_xn, do_q, do_stats, dummy):
            G = st_g
            tp_done, ev_done, acc_of = {}, {}, {}
            evv_done = {}
            GRP = 1
            oav = G["out_add_v"]; odo = G["out_dma_o"]
            for ti in range(NT):
                gidx = bi * NT + ti
                buf = ti % 3
                if ti >= 3:
                    if do_xn or do_q:
                        W("sync", "sT", tp_done[ti - 3])
                    if do_stats:
                        W("sync", "sS", acc_of[ti - 3])
                        if (mode == "cc" or (dbg and _orch == "fused2")) and not (do_xn or do_q):
                            W("sync", "sV", G["stream_mx"])
                else:
                    if ti < 2:
                        W("sync", "sV", max(oav[0], oav[1], G["stream_tail"]["sV"]))
                        W("sync", "sO", max(odo[0], odo[1]))
                    else:
                        W("sync", "sV", G["stream_tail"]["sV"])
                    W("sync", "sS", G["stream_tail"]["sS"])
                    W("sync", "sT", G["stream_tail"]["sT"])
                E("sync", "dma_start", inc="sQ", amt=16,
                  out=x_in4[buf], in_=X.ap()[bi, ti * 128:(ti + 1) * 128, :])
                q_here = em.v("sQ")
                if do_stats:
                    W("scalar", "sQ", q_here)
                    if ti == 0:
                        W("scalar", "sV", G["ep_p_written"])
                    E("scalar", "activation", dummy, x_in4[buf], AF.Identity,
                      accum_out=msum[:, gidx:gidx + 1])
                    E("scalar", "activation", dummy, x_in4[buf], AF.Square,
                      accum_out=sqsum[:, gidx:gidx + 1], inc="sS")
                    s_acc = em.v("sS")
                    acc_of[ti] = s_acc
                    G["acc_done"][bi] = s_acc
                    if mode == "cc" or (dbg and _orch == "fused2"):
                        if not (do_xn or do_q):
                            W("vector", "sQ", q_here)
                            E("vector", "tensor_reduce", mx_a[:, gidx:gidx + 1],
                              x_in4[buf], AX.X, ALU.max)
                            E("vector", "tensor_reduce", mn_a[:, gidx:gidx + 1],
                              x_in4[buf], AX.X, ALU.min, inc="sV")
                            G["stream_mx"] = em.v("sV")
                if ti % GRP != GRP - 1:
                    continue
                g0 = gidx - (GRP - 1)
                cs = slice(g0, gidx + 1)
                if do_stats:
                    W("vector", "sS", s_acc)
                    E("vector", "tensor_scalar_mul", negmu[:, cs], msum[:, cs],
                      -1.0 / 1024.0)
                    E("vector", "tensor_scalar_mul", varv[:, cs], msum[:, cs],
                      1.0 / 32.0, inc="sV")
                    W("gpsimd", "sV", em.v("sV"))
                    E("gpsimd", "tensor_tensor", mean2[:, cs], varv[:, cs],
                      varv[:, cs], ALU.mult, inc="sP")
                    W("vector", "sP", em.v("sP"))
                    E("vector", "scalar_tensor_tensor", varv[:, cs], mean2[:, cs],
                      -1.0, sqsum[:, cs], ALU.mult, ALU.add, inc="sV")
                    W("scalar", "sV", em.v("sV"))
                    E("scalar", "activation", sd_a[:, cs], varv[:, cs], AF.Sqrt,
                      scale=1.0 / 1024.0, bias=eps_b[:, 0:1], inc="sS")
                    W("vector", "sS", em.v("sS"))
                    E("vector", "reciprocal", inv_a[:, cs], sd_a[:, cs], inc="sV")
                    W("scalar", "sV", em.v("sV"))
                    E("scalar", "copy", mean2[:, cs], inv_a[:, cs], inc="sS")
                    W("vector", "sS", em.v("sS"))
                if do_q:
                    if g0 == bi * NT:
                        W("vector", "sS", rc["s_rb"])
                        if not do_stats:
                            W("vector", "sS", G["stream_tail"]["sS"])
                    E("vector", "tensor_scalar", invr_a[:, cs], mean2[:, cs],
                      rb[:, 0:1], 0.0, ALU.mult, ALU.add, inc="sV")
                for tj in ([] if not (do_xn or do_q) else range(ti - (GRP - 1), ti + 1)):
                    gj = bi * NT + tj
                    bj = tj % 3
                    if tj >= 1:
                        W("vector", "sT", tp_done[tj - 1])
                    else:
                        W("vector", "sO", max(odo[0], odo[1]))
                        W("vector", "sT", max(G["prev_epi"]["sT"], t_qw_last))
                    nops = 0
                    if do_xn:
                        E("vector", "tensor_scalar", xn32_st, x_in4[bj],
                          negmu[:, gj:gj + 1], mean2[:, gj:gj + 1],
                          ALU.add, ALU.mult, inc="sV")
                        nops += 1
                    if do_q:
                        E("vector", "tensor_scalar", q32_st, x_in4[bj],
                          negmu[:, gj:gj + 1], invr_a[:, gj:gj + 1],
                          ALU.add, ALU.mult)
                        E("vector", "tensor_scalar", x_in4[bj], q32_st, MAGIC,
                          -MAGIC, ALU.add, ALU.add, inc="sV")
                    v_tr = em.v("sV")
                    W("tensor", "sV", max(v_tr, evv_done.get(tj - 1, 0)))
                    if tj >= 1:
                        W("tensor", "sS", ev_done[tj - 1])
                    else:
                        W("tensor", "sS", max(s_qwT, em.v("sS")))
                        W("tensor", "sV", em.v("sV"))
                    if do_xn:
                        for k in range(KT):
                            E("tensor", "transpose",
                              pt[0][:, k * 128:(k + 1) * 128],
                              xn32_st[:, k * 128:(k + 1) * 128], eye32[:],
                              inc=("sT" if (k == KT - 1 and not do_q) else None))
                    if do_q:
                        for k in range(KT):
                            E("tensor", "transpose",
                              pt[1][:, k * 128:(k + 1) * 128],
                              x_in4[bj][:, k * 128:(k + 1) * 128], eye32[:],
                              inc=("sT" if k == KT - 1 else None))
                    tp_done[tj] = em.v("sT")
                    if do_xn:
                        W("vector", "sT", tp_done[tj])
                        E("vector", "tensor_copy",
                          xn_bfT.ap().rearrange("p (k n) -> p k n", k=KT)[:, :, tj * 128:(tj + 1) * 128],
                          pt[0].ap().rearrange("p (k n) -> p k n", k=KT), inc="sV")
                        evv_done[tj] = em.v("sV")
                    if do_q:
                        W("scalar", "sT", tp_done[tj])
                        E("scalar", "copy",
                          q_bfT.ap().rearrange("p (k n) -> p k n", k=KT)[:, :, tj * 128:(tj + 1) * 128],
                          pt[1].ap().rearrange("p (k n) -> p k n", k=KT), inc="sS")
                    ev_done[tj] = em.v("sS")
            if do_xn:
                G["p0_done"][bi] = {"sS": em.v("sS"), "sV": em.v("sV")}
            if do_q:
                G["q_done"][bi] = em.v("sS")
            G["stream_tail"] = {"sS": em.v("sS") if do_stats else G["stream_tail"]["sS"],
                                "sT": (max(tp_done.values()) if tp_done else
                                       G["stream_tail"]["sT"]),
                                "sV": em.v("sV")}

        def make_issuers():
            G = st_g
            wcounts, wocounts = {}, {}

            def issue_w(g):
                d_i, ptl = seq[g]
                slot = g % 3
                W("gpsimd", "sT", G["wslot_hist"][slot])
                for wt, dst in (("z", wz_blk), ("h", wh_blk)):
                    E("gpsimd", "dma_start", inc="sW", amt=16,
                      out=dst.ap().rearrange("p (b k n) -> p b k n", b=3, k=KT)[:, slot],
                      in_=Wdr[(wt, d_i)].ap().rearrange(
                          "(k p) n -> p k n", p=P)[:, :, ptl * 128:(ptl + 1) * 128])
                wcounts[g] = em.v("sW")

            def issue_wo(m):
                slot = m % 3
                W("gpsimd", "sT", G["woslot_hist"][slot])
                E("gpsimd", "dma_start", inc="sW", amt=16,
                  out=wo_blk.ap().rearrange("p (b k n) -> p b k n", b=3, k=16)[:, slot],
                  in_=WOt.ap().rearrange("(k p) n -> p k n", p=P)[:, :, m * 128:(m + 1) * 128])
                wocounts[m] = em.v("sW")

            return issue_w, issue_wo, wcounts, wocounts

        def pre_issue(bi, half):
            iw, io, wc, woc = make_issuers()
            iw(0)
            iw(1)
            io(0)
            io(1)
            st_g["pre_issued"][(bi, half)] = (iw, io, wc, woc)

        def p1_phase(bi, half):
            G = st_g
            if half == 0:
                G["carry_cnt"] = {}
            carry_cnt = G["carry_cnt"]
            scan_cnt = {}
            last_scan_v = em.v("sV")
            p0 = G["p0_done"][bi]
            p0_sS, p0_sV = p0["sS"], p0["sV"]
            kbanks = [pt[0].ap()[:, 0:512], pt[0].ap()[:, 512:1024],
                      pk[0].ap(), pk[1].ap()]
            vbanks = [pt[1].ap()[:, 0:512], pt[1].ap()[:, 512:1024],
                      pv[0].ap(), pv[1].ap()]
            drain_pool = {i: em.v("sP") for i in range(4)}
            drain_az = {i: em.v("sS") for i in range(4)}
            fz = G.get("bank_free")
            if fz is None:
                drain_tv = {i: em.v("sV") for i in range(4)}
            else:
                pkv = max(fz["pk"], fz["pv"])
                drain_tv = {0: fz["pt"], 1: fz["pt"], 2: pkv, 3: pkv}
            pre = G["pre_issued"].pop((bi, half), None)
            if pre is None:
                issue_w, issue_wo, wcounts, wocounts = make_issuers()
                issue_w(0)
                issue_w(1)
                issue_wo(0)
                issue_wo(1)
            else:
                issue_w, issue_wo, wcounts, wocounts = pre
            for gi, (dir_i, ptile) in enumerate(seq):
                par = gi % 2
                wbuf = gi % 3
                if gi + 2 < len(seq):
                    issue_w(gi + 2)
                wcount = wcounts[gi]
                for piece in range(2):
                    rot = (gi * 2 + piece) % 4
                    kb, vb = kbanks[rot], vbanks[rot]
                    if dir_i == 0:
                        t0 = half * 1024 + piece * 512
                        rhs_sl = [xn_bfT[:, k * S + t0: k * S + t0 + 512]
                                  for k in range(KT)]
                    else:
                        t0 = S - (half * 1024 + piece * 512) - 512
                        rhs_sl = [xn_bfT[:, k * S + t0: k * S + t0 + 512][:, ::-1]
                                  for k in range(KT)]
                    W("tensor", "sW", wcount)
                    W("tensor", "sS", max(drain_az[rot], p0_sS))
                    W("tensor", "sV", max(drain_tv[rot], p0_sV))
                    for k in range(KT):
                        E("tensor", "matmul", kb,
                          wz_blk.ap()[:, (wbuf * KT + k) * 128:(wbuf * KT + k + 1) * 128],
                          rhs_sl[k], start=(k == 0), stop=(k == KT - 1),
                          inc=("sT" if k == KT - 1 else None))
                    t_k = em.v("sT")
                    for k in range(KT):
                        E("tensor", "matmul", vb,
                          wh_blk.ap()[:, (wbuf * KT + k) * 128:(wbuf * KT + k + 1) * 128],
                          rhs_sl[k], start=(k == 0), stop=(k == KT - 1),
                          inc=("sT" if k == KT - 1 else None))
                    t_v = em.v("sT")
                    co = piece * 512
                    W("scalar", "sT", t_k)
                    W("scalar", "sV",
                      scan_cnt.get(seq[gi - 2], last_scan_v) if gi >= 2 else last_scan_v)
                    W("scalar", "sP", drain_pool[piece])
                    E("scalar", "activation", a_arr[par][:, co:co + 512],
                      kb, AF.Sigmoid, scale=-1.0,
                      bias=negbz[:, dir_i * KT + ptile: dir_i * KT + ptile + 1])
                    E("scalar", "activation", z_bufs[piece][:],
                      kb, AF.Sigmoid,
                      bias=biases[:, dir_i * KT + ptile: dir_i * KT + ptile + 1],
                      inc="sS")
                    W("scalar", "sT", t_v)
                    E("scalar", "activation", sv_bufs[piece][:],
                      vb, AF.Sigmoid,
                      bias=biases[:, (2 + dir_i) * KT + ptile: (2 + dir_i) * KT + ptile + 1],
                      inc="sS")
                    drain_az[rot] = em.v("sS")
                    W("vector", "sS", drain_az[rot])
                    W("vector", "sP", em.v("sP"))
                    E("vector", "scalar_tensor_tensor", t_buf[:],
                      vb,
                      bh05[:, dir_i * KT + ptile: dir_i * KT + ptile + 1],
                      sv_bufs[piece][:], ALU.add, ALU.subtract)
                    E("vector", "scalar_tensor_tensor", gh_buf[:],
                      t_buf[:], 0.0, sv_bufs[piece][:], ALU.max, ALU.add,
                      inc="sV")
                    drain_tv[rot] = em.v("sV")
                    W("gpsimd", "sV", drain_tv[rot])
                    E("gpsimd", "tensor_tensor", b_arr[par][:, co:co + 512],
                      z_bufs[piece][:], gh_buf[:], ALU.mult, inc="sP")
                    drain_pool[piece] = em.v("sP")
                G["wslot_hist"][wbuf] = em.v("sT")
                j = dir_i * KT + ptile
                W("vector", "sP", drain_pool[1])
                if half == 0:
                    W("vector", "sT", max(G["prev_epi"]["sT"], G["epi_smm_done"]))
                    E("vector", "tensor_tensor_scan",
                      h_bf[:, j * 1024:(j + 1) * 1024],
                      a_arr[par][:], b_arr[par][:], 0.0, ALU.mult, ALU.add,
                      inc="sV")
                else:
                    W("vector", "sS", carry_cnt[(dir_i, ptile)])
                    W("vector", "sT", G["epi_smm_done"])
                    E("vector", "tensor_tensor_scan",
                      h_bf[:, j * 1024:(j + 1) * 1024],
                      a_arr[par][:], b_arr[par][:], carry[:, j:j + 1],
                      ALU.mult, ALU.add, inc="sV")
                scan_cnt[(dir_i, ptile)] = em.v("sV")
                last_scan_v = em.v("sV")
                if half == 0:
                    W("scalar", "sV", last_scan_v)
                    E("scalar", "copy", carry[:, j:j + 1],
                      h_bf[:, (j + 1) * 1024 - 1:(j + 1) * 1024], inc="sS")
                    carry_cnt[(dir_i, ptile)] = em.v("sS")
            G["all_scans"][(bi, half)] = last_scan_v
            G["p1_state"] = (drain_az, drain_tv, wocounts, issue_wo)

        def epi_phase(bi, half):
            G = st_g
            drain_az, drain_tv, wocounts, issue_wo = G["p1_state"]
            all_scans_v = G["all_scans"][(bi, half)]
            evac_s = {0: drain_az[2], 1: drain_az[3]}
            evac_f = {0: drain_tv[2], 1: drain_tv[3]}
            p_written = 0
            gg_done = em.v("sP")
            q_sS = G["q_done"][bi]
            for m in range(KT):
                wobuf = m % 3
                if m + 2 < KT:
                    issue_wo(m + 2)
                wo_here = wocounts[m]
                for piece in range(2):
                    co = piece * 512
                    W("tensor", "sV", evac_f[piece])
                    if m == 0:
                        W("tensor", "sS", q_sS)
                    t0 = half * 1024 + co
                    for k in range(KT):
                        E("tensor", "matmul", pv[piece][:],
                          qwT.ap()[:, (k * KT + m) * 128:(k * KT + m + 1) * 128],
                          q_bfT[:, k * S + t0: k * S + t0 + 512],
                          start=(k == 0), stop=(k == KT - 1),
                          inc=("sT" if k == KT - 1 else None))
                    t_f = em.v("sT")
                    W("tensor", "sW", wo_here)
                    W("tensor", "sV", all_scans_v)
                    W("tensor", "sS", evac_s[piece])
                    for j in range(16):
                        E("tensor", "matmul", pk[piece][:],
                          wo_blk.ap()[:, (wobuf * 16 + j) * 128:(wobuf * 16 + j + 1) * 128],
                          h_bf[:, j * 1024 + co: j * 1024 + co + 512],
                          start=(j == 0), stop=(j == 15),
                          inc=("sT" if j == 15 else None))
                    t_s = em.v("sT")
                    W("scalar", "sT", t_s)
                    if m >= 1:
                        W("scalar", "sV", p_written)
                    E("scalar", "activation", s_m[:, co:co + 512],
                      pk[piece][:], AF.Identity,
                      bias=biases[:, 4 * KT + m:4 * KT + m + 1], inc="sS")
                    evac_s[piece] = em.v("sS")
                    W("vector", "sT", t_f)
                    W("vector", "sP", gg_done)
                    E("vector", "tensor_scalar", ffn_m[:, co:co + 512],
                      pv[piece][:], rb[:, 1:2],
                      biases[:, 5 * KT + m:5 * KT + m + 1],
                      ALU.mult, ALU.add, inc="sV")
                    evac_f[piece] = em.v("sV")
                W("scalar", "sV", evac_f[1])
                W("scalar", "sP", gg_done)
                E("scalar", "activation", sig_m[:], ffn_m[:], AF.Sigmoid,
                  inc="sS")
                s_sig = em.v("sS")
                W("gpsimd", "sS", s_sig)
                E("gpsimd", "tensor_tensor", gg_m[:], ffn_m[:], sig_m[:],
                  ALU.mult, inc="sP")
                gg_done = em.v("sP")
                W("vector", "sP", gg_done)
                W("vector", "sS", evac_s[1])
                E("vector", "tensor_tensor", p_half[:, m * 1024:(m + 1) * 1024],
                  s_m[:], gg_m[:], ALU.mult, inc="sV")
                p_written = em.v("sV")
                G["woslot_hist"][wobuf] = em.v("sT")
            G["epi_smm_done"] = em.v("sT")
            G["ep_p_written"] = p_written

            v_now = em.v("sV")
            p_now = em.v("sP")
            s_now = em.v("sS")
            oav = G["out_add_v"]; odo = G["out_dma_o"]
            for tt in range(KT):
                tb = tt % 2
                W("sync", "sV", max(oav[tb], v_now))
                W("sync", "sP", p_now)
                W("sync", "sS", s_now)
                E("sync", "dma_start", inc="sQ", amt=16,
                  out=x_epi[tb],
                  in_=X.ap()[bi, half * 1024 + tt * 128: half * 1024 + (tt + 1) * 128, :])
                q_x = em.v("sQ")
                dsts = pk if tb == 0 else pv
                W("tensor", "sV", max(p_written, oav[tb]))
                W("tensor", "sS", em.v("sS"))
                for m in range(KT):
                    E("tensor", "transpose",
                      dsts[m // 4][:, (m % 4) * 128:(m % 4 + 1) * 128],
                      p_half[:, m * 1024 + tt * 128: m * 1024 + tt * 128 + 128],
                      eye32[:], inc=("sT" if m == KT - 1 else None))
                t_tp = em.v("sT")
                W("vector", "sT", t_tp)
                W("vector", "sQ", q_x)
                W("vector", "sO", odo[tb])
                E("vector", "tensor_tensor", out_sb[tb][:, 0:512],
                  dsts[0].ap(), x_epi[tb][:, 0:512], ALU.add)
                E("vector", "tensor_tensor", out_sb[tb][:, 512:1024],
                  dsts[1].ap(), x_epi[tb][:, 512:1024], ALU.add, inc="sV")
                oav[tb] = em.v("sV")
                W("sync", "sV", oav[tb])
                E("sync", "dma_start", inc="sO", amt=16,
                  out=OUT.ap()[bi, half * 1024 + tt * 128: half * 1024 + (tt + 1) * 128, :],
                  in_=out_sb[tb])
                odo[tb] = em.v("sO")
                G["n_out_dma"] += 1
            G["bank_free"] = {"pt": p_written, "pk": oav[0], "pv": oav[1]}
            G["prev_epi"] = {"sV": em.v("sV"), "sO": em.v("sO"), "sT": em.v("sT"),
                             "sS": em.v("sS"), "sP": em.v("sP")}

        import os
        _orch = os.environ.get("KORCH", "serial")
        if mode == "cc":
            _orch = "fused"
        if _orch in ("fused", "fused2"):
            stream_batch(0, False, False, True, acc_dummy_s)
            stream_batch(1, False, False, True, acc_dummy_s)
            if mode == "cc":
                emit_gmax_cc()
            elif dbg:
                emit_localmax(sc11[:1, 4:5])
            pre_issue(0, 0)
            stream_batch(0, True, True, True, acc_dummy_s)
            p1_phase(0, 0)
            epi_phase(0, 0)
            p1_phase(0, 1)
            epi_phase(0, 1)
            pre_issue(1, 0)
            stream_batch(1, True, True, True, acc_dummy_s)
            p1_phase(1, 0)
            epi_phase(1, 0)
            p1_phase(1, 1)
            epi_phase(1, 1)
        elif _orch == "serial":
            pre_issue(0, 0)
            stream_batch(0, True, True, True, acc_dummy_s)
            p1_phase(0, 0)
            epi_phase(0, 0)
            p1_phase(0, 1)
            epi_phase(0, 1)
            pre_issue(1, 0)
            stream_batch(1, True, True, True, acc_dummy_s)
            p1_phase(1, 0)
            epi_phase(1, 0)
            p1_phase(1, 1)
            epi_phase(1, 1)
        elif _orch == "semi4":
            stream_batch(0, True, True, True, acc_dummy_s)
            p1_phase(0, 0)
            epi_phase(0, 0)
            p1_phase(0, 1)
            epi_phase(0, 1)
            stream_batch(1, False, False, True, acc_dummy_s)
            stream_batch(1, True, True, False, acc_dummy_s)
            p1_phase(1, 0)
            epi_phase(1, 0)
            p1_phase(1, 1)
            epi_phase(1, 1)
        else:
            raise ValueError(_orch)
        n_out_dma = st_g["n_out_dma"]

        W("sync", "sO", 16 * n_out_dma)
        if dbg:
            DXN = nc.dram_tensor("DXN", [P, KT * S], F32, kind="ExternalOutput")
            DQ = nc.dram_tensor("DQ", [P, KT * S], F32, kind="ExternalOutput")
            DQW = nc.dram_tensor("DQW", [P, KT * D], F32, kind="ExternalOutput")
            DRB = nc.dram_tensor("DRB", [P, 2], F32, kind="ExternalOutput")
            DGM = nc.dram_tensor("DGM", [1, 1], F32, kind="ExternalOutput")
            DH = nc.dram_tensor("DH", [P, 16 * 1024], F32, kind="ExternalOutput")
            DPH = nc.dram_tensor("DPH", [P, 8192], F32, kind="ExternalOutput")
            DSM = nc.dram_tensor("DSM", [P, 1024], F32, kind="ExternalOutput")
            DFM = nc.dram_tensor("DFM", [P, 1024], F32, kind="ExternalOutput")
            DAS = nc.dram_tensor("DAS", [P, 1], F32, kind="ExternalOutput")
            DNM = nc.dram_tensor("DNM", [P, 32], F32, kind="ExternalOutput")
            DIV = nc.dram_tensor("DIV", [P, 32], F32, kind="ExternalOutput")
            DIR = nc.dram_tensor("DIR", [P, 32], F32, kind="ExternalOutput")
            DGR = nc.dram_tensor("DGR", [1, P], F32, kind="ExternalOutput")
            DSC = nc.dram_tensor("DSC", [1, 8], F32, kind="ExternalOutput")
            for dst, srcap in ((DXN, xn_bfT.ap()), (DQ, q_bfT.ap()),
                               (DQW, qwT.ap()), (DRB, rb.ap()), (DGM, gmax[:1, :]),
                               (DH, h_bf), (DPH, p_half), (DSM, s_m), (DFM, ffn_m),
                               (DAS, asum.ap()), (DGR, grow[:1, :]), (DSC, sc11[:1, :]),
                               (DNM, negmu.ap()), (DIV, inv_a.ap()), (DIR, invr_a.ap())):
                E("gpsimd", "dma_start", inc="sG", amt=16, out=dst.ap()[:], in_=srcap)
            W("gpsimd", "sG", em.v("sG"))

        blk = st.enter_context(nc.Block())
        plan = em.plan

        @blk.sync
        def _(e):
            for f in plan["sync"]:
                f(e)

        @blk.vector
        def _(e):
            for f in plan["vector"]:
                f(e)

        @blk.scalar
        def _(e):
            for f in plan["scalar"]:
                f(e)

        @blk.tensor
        def _(e):
            for f in plan["tensor"]:
                f(e)

        @blk.gpsimd
        def _(e):
            for f in plan["gpsimd"]:
                f(e)

    return nc


_NC_CACHE = {}


def build_stats():
    nc = bass.Bass("TRN2", target_bir_lowering=False)
    X = nc.dram_tensor("X", [B_LOC, S, D], F32, kind="ExternalInput")
    OUTL = nc.dram_tensor("OUT_L", [1], F32, kind="ExternalOutput")
    st = contextlib.ExitStack()
    with st:
        sb = lambda n, sh, dt: st.enter_context(nc.sbuf_tensor(n, sh, dt))
        x_in = [sb(f"x{i}", [P, D], F32) for i in (0, 1)]
        accdum = sb("accdum", [P, D], F32)
        msum = sb("msum", [P, B_LOC * NT], F32)
        sqsum = sb("sqsum", [P, B_LOC * NT], F32)
        mean2 = sb("mean2", [P, B_LOC * NT], F32)
        varv = sb("varv", [P, B_LOC * NT], F32)
        negmu = sb("negmu", [P, B_LOC * NT], F32)
        sd_a = sb("sd_a", [P, B_LOC * NT], F32)
        inv_a = sb("inv_a", [P, B_LOC * NT], F32)
        mx_s = sb("mx_s", [P, B_LOC * NT], F32)
        mn_s = sb("mn_s", [P, B_LOC * NT], F32)
        am_s = sb("am_s", [P, B_LOC * NT], F32)
        lmax = sb("lmax", [P, 1], F32)
        grow = sb("grow", [1, P], F32)
        gmax = sb("gmax", [1, 1], F32)
        eps_b = sb("eps_b", [P, 1], F32)
        c127 = sb("c127", [1, 1], F32)
        grow_lm = sb("grow_lm", [1, 8], F32)
        em = Em(nc, st)
        E, W = em.op, em.wait
        E("vector", "memset", eps_b[:], 1e-5, inc="sV")
        vq, sq = [], []
        for bi in range(B_LOC):
            for ti in range(NT):
                gidx = bi * NT + ti
                buf = gidx % 2
                if gidx >= 2:
                    W("sync", "sV", vq[gidx - 2])
                    W("sync", "sS", sq[gidx - 2])
                E("sync", "dma_start", inc="sQ", amt=16,
                  out=x_in[buf][:], in_=X.ap()[bi, ti * 128:(ti + 1) * 128, :])
                q = em.v("sQ")
                W("scalar", "sQ", q)
                E("scalar", "activation", accdum[:], x_in[buf][:], AF.Identity,
                  accum_out=msum[:, gidx:gidx + 1])
                E("scalar", "activation", accdum[:], x_in[buf][:], AF.Square,
                  accum_out=sqsum[:, gidx:gidx + 1], inc="sS")
                sq.append(em.v("sS"))
                W("vector", "sQ", q)
                E("vector", "tensor_reduce", mx_s[:, gidx:gidx + 1], x_in[buf][:],
                  AX.X, ALU.max)
                E("vector", "tensor_reduce", mn_s[:, gidx:gidx + 1], x_in[buf][:],
                  AX.X, ALU.min, inc="sV")
                vq.append(em.v("sV"))
        W("vector", "sS", em.v("sS"))
        E("vector", "tensor_scalar_mul", negmu[:], msum[:], -1.0 / 1024.0, inc="sV")
        W("gpsimd", "sV", em.v("sV"))
        E("gpsimd", "tensor_tensor", mean2[:], negmu[:], negmu[:], ALU.mult,
          inc="sP")
        W("vector", "sP", em.v("sP"))
        E("vector", "scalar_tensor_tensor", varv[:], sqsum[:], 1.0 / 1024.0,
          mean2[:], ALU.mult, ALU.subtract, inc="sV")
        W("scalar", "sV", em.v("sV"))
        E("scalar", "activation", sd_a[:], varv[:], AF.Sqrt,
          bias=eps_b[:, 0:1], inc="sS")
        W("vector", "sS", em.v("sS"))
        E("vector", "reciprocal", inv_a[:], sd_a[:], inc="sV")
        E("vector", "tensor_tensor", am_s[:], mx_s[:], negmu[:], ALU.add)
        E("vector", "tensor_tensor", mx_s[:], mn_s[:], negmu[:], ALU.add, inc="sV")
        W("gpsimd", "sV", em.v("sV"))
        E("gpsimd", "tensor_tensor", mn_s[:], am_s[:], inv_a[:], ALU.mult)
        E("gpsimd", "tensor_tensor", am_s[:], mx_s[:], inv_a[:], ALU.mult,
          inc="sP")
        W("vector", "sP", em.v("sP"))
        E("vector", "scalar_tensor_tensor", mx_s[:], am_s[:], -1.0, mn_s[:],
          ALU.mult, ALU.max, inc="sV")
        W("scalar", "sV", em.v("sV"))
        E("scalar", "copy", sd_a[:], mx_s[:], inc="sS")
        W("vector", "sS", em.v("sS"))
        E("vector", "tensor_reduce", lmax[:, 0:1], sd_a[:], AX.X, ALU.max,
          inc="sV")
        v_lm = em.v("sV")
        W("gpsimd", "sV", v_lm)
        E("gpsimd", "dma_start", inc="sG", amt=16, out=grow[:1, :], in_=lmax[:, 0:1])
        W("vector", "sG", em.v("sG"))
        E("vector", "tensor_reduce", gmax[:1, 0:1], grow[:1, :], AX.X, ALU.max,
          inc="sV")
        v_gm = em.v("sV")
        W("sync", "sV", v_gm)
        E("sync", "dma_start", inc="sO", amt=16, out=OUTL.ap()[0:1],
          in_=gmax[:1, 0:1])
        W("sync", "sO", 16)
        blk = st.enter_context(nc.Block())
        plan = em.plan

        @blk.sync
        def _(e):
            for f in plan["sync"]:
                f(e)

        @blk.vector
        def _(e):
            for f in plan["vector"]:
                f(e)

        @blk.scalar
        def _(e):
            for f in plan["scalar"]:
                f(e)

        @blk.tensor
        def _(e):
            for f in plan["tensor"]:
                f(e)

        @blk.gpsimd
        def _(e):
            for f in plan["gpsimd"]:
                f(e)
    return nc



_PROBE_SRC = """
import contextlib
import numpy as np
import concourse.bass as bass
import concourse.mybir as mybir
from concourse.bass_utils import run_bass_kernel_spmd
F32 = mybir.dt.float32
ALU = mybir.AluOpType
nc = bass.Bass("TRN2", target_bir_lowering=False)
A = nc.dram_tensor("A", [1], F32, kind="ExternalInput")
O = nc.dram_tensor("O", [1, 1], F32, kind="ExternalOutput")
CI = nc.dram_tensor("p_ci", [1], F32)
CO = nc.dram_tensor("p_co", [1], F32)
with contextlib.ExitStack() as st:
    a = st.enter_context(nc.sbuf_tensor("a", [1, 1], F32))
    sG = st.enter_context(nc.semaphore("sG"))
    sO = st.enter_context(nc.semaphore("sO"))
    blk = st.enter_context(nc.Block())
    @blk.sync
    def _(s):
        s.wait_ge(sG, 64)
        s.dma_start(out=O[:1, :], in_=a[:1, 0:1]).then_inc(sO, 16)
        s.wait_ge(sO, 16)
    @blk.gpsimd
    def _(g):
        g.dma_start(out=a[:1, 0:1], in_=A[None, 0:1]).then_inc(sG, 16)
        g.wait_ge(sG, 16)
        g.dma_start(out=CI.ap()[0:1], in_=a[:1, 0:1]).then_inc(sG, 16)
        g.wait_ge(sG, 32)
        g.collective_compute("AllReduce", ALU.max, replica_groups=[list(range(8))],
                             ins=[CI.ap()[0:1]], outs=[CO.ap()[0:1]]).then_inc(sG, 16)
        g.wait_ge(sG, 48)
        g.dma_start(out=a[:1, 0:1], in_=CO.ap()[None, 0:1]).then_inc(sG, 16)
    @blk.vector
    def _(v):
        pass
    @blk.scalar
    def _(s):
        pass
    @blk.tensor
    def _(t):
        pass
ins = [dict(A=np.array([float(c)], np.float32)) for c in range(8)]
res = run_bass_kernel_spmd(nc, ins, core_ids=list(range(8)))
ok = all(abs(res.results[c]["O"][0, 0] - 7.0) < 1e-6 for c in range(8))
print("PROBE_OK" if ok else "PROBE_BAD")
"""

_PROBE_CACHE = {}


def _probe_collectives():
    if "ok" not in _PROBE_CACHE:
        import subprocess
        import sys
        try:
            r = subprocess.run([sys.executable, "-c", _PROBE_SRC],
                               capture_output=True, text=True, timeout=600)
            _PROBE_CACHE["ok"] = "PROBE_OK" in r.stdout
        except Exception:
            _PROBE_CACHE["ok"] = False
    return _PROBE_CACHE["ok"]


def kernel(**inputs):
    import os
    x = np.ascontiguousarray(inputs["L0_attn_x"])
    n = 8
    shards = [np.ascontiguousarray(x[i * B_LOC:(i + 1) * B_LOC]) for i in range(n)]
    base = dict(
        WZ_f=np.ascontiguousarray(inputs["L0_attn_Wz_f"]),
        WZ_b=np.ascontiguousarray(inputs["L0_attn_Wz_b"]),
        WH_f=np.ascontiguousarray(inputs["L0_attn_Wh_f"]),
        WH_b=np.ascontiguousarray(inputs["L0_attn_Wh_b"]),
        BZ_f=np.ascontiguousarray(inputs["L0_attn_bz_f"]),
        BZ_b=np.ascontiguousarray(inputs["L0_attn_bz_b"]),
        BH_f=np.ascontiguousarray(inputs["L0_attn_bh_f"]),
        BH_b=np.ascontiguousarray(inputs["L0_attn_bh_b"]),
        WO=np.ascontiguousarray(inputs["L0_attn_W_out"]),
        BO=np.ascontiguousarray(inputs["L0_attn_b_out"]),
        FW=np.ascontiguousarray(inputs["L0_ffn_weight"]),
        FB=np.ascontiguousarray(inputs["L0_ffn_bias"]),
        EYE=np.eye(P, dtype=np.float32),
    )
    kmode = os.environ.get("KMODE", "auto")
    if kmode in ("auto", "cc"):
        if kmode == "cc" or _probe_collectives():
            if "cc" not in _NC_CACHE:
                _NC_CACHE["cc"] = build_nc("cc")
            res = run_bass_kernel_spmd(_NC_CACHE["cc"],
                                       [dict(base, X=s) for s in shards],
                                       core_ids=list(range(n)))
            return np.concatenate([r["OUT"] for r in res.results], axis=0)
    if "stats" not in _NC_CACHE:
        _NC_CACHE["stats"] = build_stats()
    if "main" not in _NC_CACHE:
        _NC_CACHE["main"] = build_nc("lmax8")

    def _run(ncx, maps):
        last = None
        for _ in range(3):
            try:
                return run_bass_kernel_spmd(ncx, maps, core_ids=list(range(n)))
            except Exception as e:
                last = e
        raise last

    resA = _run(_NC_CACHE["stats"], [dict(X=s) for s in shards])
    lm8 = np.concatenate([r["OUT_L"] for r in resA.results]).astype(np.float32)
    in_maps = [dict(base, X=s, LMAX8=lm8) for s in shards]
    res = _run(_NC_CACHE["main"], in_maps)
    return np.concatenate([r["OUT"] for r in res.results], axis=0)


if __name__ == "__main__":
    nc = build_nc("lmax8")
    print("build ok; instrs:",
          sum(len(b.instructions) for f in nc.m.functions for b in f.blocks))


# version 45 (final)
# speedup vs baseline: 1.1323x; 1.1323x over previous
import contextlib
import numpy as np
import concourse.bass as bass
import concourse.mybir as mybir
from concourse.bass_utils import run_bass_kernel_spmd

F32 = mybir.dt.float32
BF16 = mybir.dt.bfloat16
U8 = mybir.dt.uint8
AF = mybir.ActivationFunctionType
ALU = mybir.AluOpType
AX = mybir.AxisListType

P = 128
B_LOC = 2
S = 2048
D = 1024
KT = 8
NT = 16
MAGIC = float(1.5 * 2.0 ** 23)

SEMS = ("sQ", "sW", "sG", "sT", "sV", "sS", "sP", "sO")


class Em:

    def __init__(self, nc, stack):
        self.nc = nc
        self.plan = {e: [] for e in ("sync", "vector", "scalar", "tensor", "gpsimd")}
        self.cnt = {s: 0 for s in SEMS}
        self.sem = {s: stack.enter_context(nc.semaphore(s)) for s in SEMS}

    def op(self, eng, method, *args, inc=None, amt=1, **kw):
        if inc is not None:
            self.cnt[inc] += amt
        sem = self.sem

        def fn(e, m=method, a=args, k=kw, i=inc, n=amt):
            r = getattr(e, m)(*a, **k)
            if i is not None:
                r.then_inc(sem[i], n)
        self.plan[eng].append(fn)
        return self.cnt[inc] if inc is not None else None

    def wait(self, eng, s, val):
        assert val is not None and val <= self.cnt[s], (eng, s, val, self.cnt[s])
        sem = self.sem
        self.plan[eng].append(lambda e, ss=s, v=val: e.wait_ge(sem[ss], v))

    def v(self, s):
        return self.cnt[s]


def build_nc(mode="lmax8", dbg=False):
    nc = bass.Bass("TRN2", target_bir_lowering=False)

    X = nc.dram_tensor("X", [B_LOC, S, D], F32, kind="ExternalInput")
    Wdr = {("z", 0): nc.dram_tensor("WZ_f", [D, D], F32, kind="ExternalInput"),
           ("z", 1): nc.dram_tensor("WZ_b", [D, D], F32, kind="ExternalInput"),
           ("h", 0): nc.dram_tensor("WH_f", [D, D], F32, kind="ExternalInput"),
           ("h", 1): nc.dram_tensor("WH_b", [D, D], F32, kind="ExternalInput")}
    BZf = nc.dram_tensor("BZ_f", [D], F32, kind="ExternalInput")
    BZb = nc.dram_tensor("BZ_b", [D], F32, kind="ExternalInput")
    BHf = nc.dram_tensor("BH_f", [D], F32, kind="ExternalInput")
    BHb = nc.dram_tensor("BH_b", [D], F32, kind="ExternalInput")
    WOt = nc.dram_tensor("WO", [2 * D, D], F32, kind="ExternalInput")
    BOt = nc.dram_tensor("BO", [D], F32, kind="ExternalInput")
    FWt = nc.dram_tensor("FW", [D, D], F32, kind="ExternalInput")
    FBt = nc.dram_tensor("FB", [D], F32, kind="ExternalInput")
    EYEt = nc.dram_tensor("EYE", [P, P], F32, kind="ExternalInput")
    OUT = nc.dram_tensor("OUT", [B_LOC, S, D], F32, kind="ExternalOutput")
    if mode == "cc":
        CCI = nc.dram_tensor("cc_in", [1], F32)
        CCO = nc.dram_tensor("cc_out", [1], F32)
    else:
        LM8 = nc.dram_tensor("LMAX8", [8], F32, kind="ExternalInput")

    st = contextlib.ExitStack()
    with st:
        sb = lambda n, sh, dt: st.enter_context(nc.sbuf_tensor(n, sh, dt))

        xn_bfT = sb("xn_bfT", [P, KT * S], BF16)
        q_bfT = sb("q_bfT", [P, KT * S], BF16)
        qwT = sb("qwT", [P, KT * D], BF16)
        wz_blk = sb("wz_blk", [P, 3 * KT * P], BF16)
        wh_blk = sb("wh_blk", [P, 3 * KT * P], BF16)
        wo_blk = sb("wo_blk", [P, 3 * 16 * P], BF16)
        g1 = sb("g1", [P, 32 * 1024], U8)
        g2 = sb("g2", [P, 48 * 1024], U8)
        g3 = sb("g3", [P, 20 * 1024], U8)
        msum = sb("msum", [P, B_LOC * NT], F32)
        sqsum = sb("sqsum", [P, B_LOC * NT], F32)
        mean2 = sb("mean2", [P, B_LOC * NT], F32)
        varv = sb("varv", [P, B_LOC * NT], F32)
        negmu = sb("negmu", [P, B_LOC * NT], F32)
        inv_a = sb("inv_a", [P, B_LOC * NT], F32)
        invr_a = sb("invr_a", [P, B_LOC * NT], F32)
        lmax = sb("lmax", [P, 1], F32)
        mx_a = sb("mx_a", [P, B_LOC * NT], F32)
        mn_a = sb("mn_a", [P, B_LOC * NT], F32)
        sd_a = sb("sd_a", [P, B_LOC * NT], F32)
        asum = sb("asum", [P, 1], F32)
        grow = sb("grow", [1, P], F32)
        gmax = sb("gmax", [1, 1], F32)
        sc11 = sb("sc11", [1, 8], F32)
        ones_r = sb("ones_r", [1, P], F32)
        rb = sb("rb", [P, 2], F32)
        thb = sb("thb", [P, 1], F32)
        negth = sb("negth", [P, 1], F32)
        biases = sb("biases", [P, 6 * KT], F32)
        negbz = sb("negbz", [P, 2 * KT], F32)
        bh05 = sb("bh05", [P, 2 * KT], F32)
        carry = sb("carry", [P, 16], F32)
        eye32 = sb("eye32", [P, P], F32)
        eye_bf = sb("eye_bf", [P, P], BF16)
        eps_b = sb("eps_b", [P, 1], F32)
        c127 = sb("c127", [1, 1], F32)
        grow_lm = sb("grow_lm", [1, 8], F32)

        bc = lambda ap, dt: ap.bitcast(dt)
        fw32 = bc(g1.ap()[:, 0:32768], F32)
        h_bf = bc(g1.ap()[:, 0:32768], BF16)
        sign_s = bc(g2.ap()[:, 0:16384], BF16)
        mask_s = bc(g2.ap()[:, 16384:32768], BF16)
        mask2_s = bc(g2.ap()[:, 32768:49152], BF16)
        a_arr = [bc(g2.ap()[:, i * 4096:(i + 1) * 4096], F32) for i in (0, 1)]
        b_arr = [bc(g2.ap()[:, i * 4096:(i + 1) * 4096], F32) for i in (2, 3)]
        z_bufs = [bc(g2.ap()[:, 16384 + i * 2048: 16384 + (i + 1) * 2048], F32) for i in (0, 1)]
        sv_bufs = [bc(g2.ap()[:, 20480 + i * 2048: 20480 + (i + 1) * 2048], F32) for i in (0, 1)]
        t_buf = bc(g2.ap()[:, 24576:26624], F32)
        gh_buf = bc(g2.ap()[:, 26624:28672], F32)
        p_half = bc(g2.ap()[:, 32768:49152], BF16)
        s_m = bc(g2.ap()[:, 0:4096], F32)
        ffn_m = bc(g2.ap()[:, 4096:8192], F32)
        sig_m = bc(g2.ap()[:, 8192:12288], F32)
        gg_m = bc(g2.ap()[:, 12288:16384], F32)
        x_in4 = [bc(g3.ap()[:, i * 4096:(i + 1) * 4096], F32) for i in range(3)]
        qw_f = bc(g3.ap()[:, 12288:16384], F32)
        xn32_st = bc(g3.ap()[:, 12288:16384], F32)
        q32_st = bc(g3.ap()[:, 16384:20480], F32)
        x_epi = [x_in4[0], x_in4[1]]
        out_sb = [bc(g3.ap()[:, 12288:16384], F32), bc(g3.ap()[:, 16384:20480], F32)]

        pk = [st.enter_context(nc.psum_tensor(f"pk{i}", [P, 512], F32)) for i in (0, 1)]
        pv = [st.enter_context(nc.psum_tensor(f"pv{i}", [P, 512], F32)) for i in (0, 1)]
        pt = [st.enter_context(nc.psum_tensor(f"pt{i}", [P, 1024], F32)) for i in (0, 1)]

        em = Em(nc, st)
        E, W = em.op, em.wait

        E("sync", "dma_start", inc="sQ", amt=16, out=eye32[:], in_=EYEt[:])
        for j, bt in enumerate((BZf, BZb, BHf, BHb, BOt, FBt)):
            E("sync", "dma_start", inc="sQ", amt=16,
              out=biases[:, j * KT:(j + 1) * KT],
              in_=bt.ap().rearrange("(k p) -> p k", p=P),
              allow_slow_non_contiguous=True)
        E("sync", "dma_start", inc="sQ", amt=16,
          out=fw32.rearrange("p (k n) -> p k n", k=KT),
          in_=FWt.ap().rearrange("(k p) n -> p k n", p=P))
        ld_const = em.v("sQ")

        E("vector", "memset", ones_r[:1, :], 1.0)
        E("vector", "memset", eps_b[:], 1e-5)
        E("vector", "memset", c127[:1, :], 127.0)
        E("vector", "memset", sc11[:1, :], 1.0, inc="sV")

        W("scalar", "sQ", ld_const)
        E("scalar", "copy", eye_bf[:], eye32[:])
        E("scalar", "activation", negbz[:], biases[:, 0:2 * KT], AF.Copy, scale=-1.0)
        E("scalar", "activation", sign_s, fw32, AF.Sign, inc="sS")
        s_sign = em.v("sS")
        W("vector", "sQ", ld_const)
        E("vector", "tensor_scalar_add", bh05[:], biases[:, 2 * KT:4 * KT], 0.5, inc="sV")

        W("vector", "sQ", ld_const)
        E("vector", "tensor_reduce", asum[:, 0:1], fw32, AX.X, ALU.add,
          apply_absolute_value=True, inc="sV")
        v_asum = em.v("sV")
        W("gpsimd", "sV", v_asum)
        E("gpsimd", "dma_start", inc="sG", amt=16, out=grow[:1, :], in_=asum[:, 0:1])
        g_asum = em.v("sG")
        W("vector", "sG", g_asum)
        E("vector", "tensor_reduce", sc11[:1, 6:7], grow[:1, :], AX.X, ALU.add,
          inc="sV")
        v_wsum = em.v("sV")
        W("scalar", "sV", v_wsum)
        E("scalar", "activation", sc11[:1, 3:4], sc11[:1, 6:7], AF.Copy,
          scale=float(2.0 ** -20))
        E("scalar", "activation", sc11[:1, 4:5], sc11[:1, 6:7], AF.Copy,
          scale=float(2.0 ** -21), inc="sS")
        s_wsth = em.v("sS")
        W("tensor", "sS", s_wsth)
        E("tensor", "matmul", pt[0][:, 0:1], ones_r[:1, :], sc11[:1, 4:5],
          start=True, stop=True, inc="sT")
        t_th = em.v("sT")
        W("scalar", "sT", t_th)
        E("scalar", "copy", thb[:, 0:1], pt[0][:, 0:1])
        E("scalar", "activation", negth[:, 0:1], pt[0][:, 0:1], AF.Copy,
          scale=-1.0, inc="sS")
        s_th = em.v("sS")
        W("vector", "sS", s_th)
        E("vector", "tensor_scalar", mask_s, fw32, thb[:, 0:1], 0.0,
          ALU.is_gt, ALU.add)
        E("vector", "tensor_scalar", mask2_s, fw32, negth[:, 0:1], 0.0,
          ALU.is_lt, ALU.add)
        E("vector", "tensor_tensor", mask_s, mask_s, mask2_s, ALU.add, inc="sV")
        v_mask = em.v("sV")

        W("gpsimd", "sS", s_sign)
        W("gpsimd", "sV", v_mask)
        qw_tp = []
        qw_ev = []
        for m in range(KT):
            if m >= 1:
                W("gpsimd", "sT", qw_tp[m - 1])
            E("gpsimd", "tensor_tensor", qw_f[:],
              sign_s[:, m * D:(m + 1) * D], mask_s[:, m * D:(m + 1) * D],
              ALU.mult, inc="sP")
            p_qw = em.v("sP")
            W("tensor", "sP", p_qw)
            if m >= 2:
                W("tensor", "sS", qw_ev[m - 2])
            for k in range(KT):
                E("tensor", "transpose", pt[m % 2][:, k * 128:(k + 1) * 128],
                  qw_f[:, k * 128:(k + 1) * 128], eye32[:],
                  inc=("sT" if k == KT - 1 else None))
            qw_tp.append(em.v("sT"))
            W("scalar", "sT", qw_tp[-1])
            E("scalar", "copy",
              qwT.ap().rearrange("p (k n) -> p k n", k=KT)[:, :, m * 128:(m + 1) * 128],
              pt[m % 2].ap().rearrange("p (k n) -> p k n", k=KT),
              inc="sS")
            qw_ev.append(em.v("sS"))
        s_qwT = em.v("sS")
        t_qw_last = qw_tp[-1]

        acc_dummy = bc(g2.ap()[:, 32768:36864], F32)
        s_inv = 0
        rc = {}

        def emit_rchain():
            v_gmax2 = em.v("sV")
            W("scalar", "sV", v_gmax2)
            if mode == "cc":
                W("scalar", "sG", em.v("sG"))
            E("scalar", "copy", sc11[:1, 7:8], gmax[:1, 0:1], inc="sS")
            s_cp1 = em.v("sS")
            W("vector", "sS", s_cp1)
            E("vector", "tensor_scalar_mul", sc11[:1, 0:1], sc11[:1, 7:8],
              float(np.float32(1.0) / np.float32(127.0)))
            E("vector", "reciprocal", sc11[:1, 6:7], sc11[:1, 7:8], inc="sV")
            v_rg = em.v("sV")
            W("gpsimd", "sV", v_rg)
            E("gpsimd", "tensor_tensor", sc11[:1, 1:2], sc11[:1, 6:7],
              c127[:1, 0:1], ALU.mult)
            E("gpsimd", "tensor_tensor", sc11[:1, 2:3], sc11[:1, 3:4],
              sc11[:1, 0:1], ALU.mult, inc="sP")
            p_wsis = em.v("sP")
            W("tensor", "sP", p_wsis)
            W("tensor", "sS", s_qwT)
            E("tensor", "matmul", pt[0][:, 0:2], ones_r[:1, :], sc11[:1, 1:3],
              start=True, stop=True, inc="sT")
            t_rb = em.v("sT")
            W("scalar", "sT", t_rb)
            E("scalar", "copy", rb[:], pt[0][:, 0:2], inc="sS")
            rc["s_rb"] = em.v("sS")

        if mode != "cc":
            E("sync", "dma_start", inc="sQ", amt=16, out=grow_lm[:1, 0:8],
              in_=LM8.ap()[None, :])
            g_lm8 = em.v("sQ")
            W("vector", "sQ", g_lm8)
            E("vector", "tensor_reduce", gmax[:1, 0:1], grow_lm[:1, 0:8], AX.X,
              ALU.max, inc="sV")
            emit_rchain()

        def emit_localmax(dst):
            E("vector", "tensor_tensor", varv[:], mx_a[:], negmu[:], ALU.add)
            E("vector", "tensor_tensor", mx_a[:], mn_a[:], negmu[:], ALU.add,
              inc="sV")
            W("gpsimd", "sV", em.v("sV"))
            E("gpsimd", "tensor_tensor", mn_a[:], varv[:], inv_a[:], ALU.mult)
            E("gpsimd", "tensor_tensor", varv[:], mx_a[:], inv_a[:], ALU.mult,
              inc="sP")
            W("vector", "sP", em.v("sP"))
            E("vector", "scalar_tensor_tensor", mx_a[:], varv[:], -1.0, mn_a[:],
              ALU.mult, ALU.max, inc="sV")
            W("scalar", "sV", em.v("sV"))
            E("scalar", "copy", sd_a[:], mx_a[:], inc="sS")
            W("vector", "sS", em.v("sS"))
            E("vector", "tensor_reduce", lmax[:, 0:1], sd_a[:], AX.X, ALU.max,
              inc="sV")
            W("gpsimd", "sV", em.v("sV"))
            E("gpsimd", "dma_start", inc="sG", amt=16, out=grow[:1, :],
              in_=lmax[:, 0:1])
            W("vector", "sG", em.v("sG"))
            E("vector", "tensor_reduce", dst, grow[:1, :], AX.X,
              ALU.max, inc="sV")

        def emit_gmax_cc():
            emit_localmax(gmax[:1, 0:1])
            W("gpsimd", "sV", em.v("sV"))
            E("gpsimd", "dma_start", inc="sG", amt=16, out=CCI.ap()[0:1],
              in_=gmax[:1, 0:1])
            W("gpsimd", "sG", em.v("sG"))
            E("gpsimd", "collective_compute", "AllReduce", ALU.max,
              replica_groups=[list(range(8))], ins=[CCI.ap()[0:1]],
              outs=[CCO.ap()[0:1]], inc="sG", amt=16)
            W("gpsimd", "sG", em.v("sG"))
            E("gpsimd", "dma_start", inc="sG", amt=16, out=gmax[:1, 0:1],
              in_=CCO.ap()[None, 0:1])
            emit_rchain()

        st_g = {
            "n_out_dma": 0,
            "prev_epi": {"sV": 0, "sO": 0, "sT": 0, "sS": 0, "sP": 0},
            "out_add_v": {0: 0, 1: 0},
            "out_dma_o": {0: 0, 1: 0},
            "wslot_hist": {0: 0, 1: 0, 2: 0},
            "woslot_hist": {0: 0, 1: 0, 2: 0},
            "pre_issued": {},
            "epi_smm_done": 0,
            "stream_tail": {"sS": 0, "sT": 0, "sV": 0},
            "xe_guard": {"sS": 0, "sT": 0, "sV": 0},
            "p0_done": {},
            "q_done": {},
            "acc_done": {},
            "carry_cnt": {},
            "all_scans": {},
            "ep_p_written": 0,
        }
        seq = [(d, ptl) for d in range(2) for ptl in range(KT)]
        acc_dummy_w = bc(wz_blk.ap()[:, 0:2048], F32)
        acc_dummy_s = bc(g2.ap()[:, 16384:20480], F32)

        def stream_batch(bi, do_xn, do_q, do_stats, dummy):
            G = st_g
            tp_done, ev_done, acc_of = {}, {}, {}
            evv_done = {}
            GRP = 2
            oav = G["out_add_v"]; odo = G["out_dma_o"]
            for ti in range(NT):
                gidx = bi * NT + ti
                buf = ti % 3
                if ti >= 3:
                    if do_xn or do_q:
                        W("sync", "sT", tp_done[ti - 3])
                    if do_stats:
                        W("sync", "sS", acc_of[ti - 3])
                        if (mode == "cc" or (dbg and _orch == "fused2")) and not (do_xn or do_q):
                            W("sync", "sV", G["stream_mx"])
                else:
                    if ti < 2:
                        W("sync", "sV", max(oav[0], oav[1], G["stream_tail"]["sV"]))
                        W("sync", "sO", max(odo[0], odo[1]))
                    else:
                        W("sync", "sV", G["stream_tail"]["sV"])
                    W("sync", "sS", G["stream_tail"]["sS"])
                    W("sync", "sT", G["stream_tail"]["sT"])
                E("sync", "dma_start", inc="sQ", amt=16,
                  out=x_in4[buf], in_=X.ap()[bi, ti * 128:(ti + 1) * 128, :])
                q_here = em.v("sQ")
                if do_stats:
                    W("scalar", "sQ", q_here)
                    if ti == 0:
                        W("scalar", "sV", G["ep_p_written"])
                    E("scalar", "activation", dummy, x_in4[buf], AF.Identity,
                      accum_out=msum[:, gidx:gidx + 1])
                    E("scalar", "activation", dummy, x_in4[buf], AF.Square,
                      accum_out=sqsum[:, gidx:gidx + 1], inc="sS")
                    s_acc = em.v("sS")
                    acc_of[ti] = s_acc
                    G["acc_done"][bi] = s_acc
                    if mode == "cc" or (dbg and _orch == "fused2"):
                        if not (do_xn or do_q):
                            W("vector", "sQ", q_here)
                            E("vector", "tensor_reduce", mx_a[:, gidx:gidx + 1],
                              x_in4[buf], AX.X, ALU.max)
                            E("vector", "tensor_reduce", mn_a[:, gidx:gidx + 1],
                              x_in4[buf], AX.X, ALU.min, inc="sV")
                            G["stream_mx"] = em.v("sV")
                if ti % GRP != GRP - 1:
                    continue
                g0 = gidx - (GRP - 1)
                cs = slice(g0, gidx + 1)
                if do_stats:
                    E("scalar", "activation", varv[:, cs], msum[:, cs], AF.Square,
                      scale=2.0 ** -5)
                    E("scalar", "activation", negmu[:, cs], msum[:, cs], AF.Copy,
                      scale=-(2.0 ** -10), inc="sS")
                    W("gpsimd", "sS", em.v("sS"))
                    E("gpsimd", "tensor_tensor", sd_a[:, cs], sqsum[:, cs],
                      varv[:, cs], ALU.subtract, inc="sP")
                    W("scalar", "sP", em.v("sP"))
                    E("scalar", "activation", varv[:, cs], sd_a[:, cs], AF.Sqrt,
                      scale=2.0 ** -10, bias=eps_b[:, 0:1], inc="sS")
                    W("vector", "sS", em.v("sS"))
                    E("vector", "reciprocal", inv_a[:, cs], varv[:, cs], inc="sV")
                    W("scalar", "sV", em.v("sV"))
                    E("scalar", "copy", mean2[:, cs], inv_a[:, cs], inc="sS")
                    W("vector", "sS", em.v("sS"))
                if do_q:
                    if g0 == bi * NT:
                        W("vector", "sS", rc["s_rb"])
                        if not do_stats:
                            W("vector", "sS", G["stream_tail"]["sS"])
                    E("vector", "tensor_scalar", invr_a[:, cs], mean2[:, cs],
                      rb[:, 0:1], 0.0, ALU.mult, ALU.add, inc="sV")
                for tj in ([] if not (do_xn or do_q) else range(ti - (GRP - 1), ti + 1)):
                    gj = bi * NT + tj
                    bj = tj % 3
                    if tj >= 1:
                        W("vector", "sT", tp_done[tj - 1])
                    else:
                        W("vector", "sO", max(odo[0], odo[1]))
                        W("vector", "sT", max(G["prev_epi"]["sT"], t_qw_last))
                    nops = 0
                    if do_xn:
                        E("vector", "tensor_scalar", xn32_st, x_in4[bj],
                          negmu[:, gj:gj + 1], mean2[:, gj:gj + 1],
                          ALU.add, ALU.mult, inc="sV")
                        nops += 1
                    if do_q:
                        E("vector", "tensor_scalar", q32_st, x_in4[bj],
                          negmu[:, gj:gj + 1], invr_a[:, gj:gj + 1],
                          ALU.add, ALU.mult)
                        E("vector", "tensor_scalar", x_in4[bj], q32_st, MAGIC,
                          -MAGIC, ALU.add, ALU.add, inc="sV")
                    v_tr = em.v("sV")
                    W("tensor", "sV", max(v_tr, evv_done.get(tj - 1, 0)))
                    if tj >= 1:
                        W("tensor", "sS", ev_done[tj - 1])
                    else:
                        W("tensor", "sS", max(s_qwT, em.v("sS")))
                        W("tensor", "sV", em.v("sV"))
                    if do_xn:
                        for k in range(KT):
                            E("tensor", "transpose",
                              pt[0][:, k * 128:(k + 1) * 128],
                              xn32_st[:, k * 128:(k + 1) * 128], eye32[:],
                              inc=("sT" if (k == KT - 1 and not do_q) else None))
                    if do_q:
                        for k in range(KT):
                            E("tensor", "transpose",
                              pt[1][:, k * 128:(k + 1) * 128],
                              x_in4[bj][:, k * 128:(k + 1) * 128], eye32[:],
                              inc=("sT" if k == KT - 1 else None))
                    tp_done[tj] = em.v("sT")
                    if do_xn:
                        W("vector", "sT", tp_done[tj])
                        E("vector", "tensor_copy",
                          xn_bfT.ap().rearrange("p (k n) -> p k n", k=KT)[:, :, tj * 128:(tj + 1) * 128],
                          pt[0].ap().rearrange("p (k n) -> p k n", k=KT), inc="sV")
                        evv_done[tj] = em.v("sV")
                    if do_q:
                        W("scalar", "sT", tp_done[tj])
                        E("scalar", "copy",
                          q_bfT.ap().rearrange("p (k n) -> p k n", k=KT)[:, :, tj * 128:(tj + 1) * 128],
                          pt[1].ap().rearrange("p (k n) -> p k n", k=KT), inc="sS")
                    ev_done[tj] = em.v("sS")
            if do_xn:
                G["p0_done"][bi] = {"sS": em.v("sS"), "sV": em.v("sV")}
            if do_q:
                G["q_done"][bi] = em.v("sS")
            G["stream_tail"] = {"sS": em.v("sS") if do_stats else G["stream_tail"]["sS"],
                                "sT": (max(tp_done.values()) if tp_done else
                                       G["stream_tail"]["sT"]),
                                "sV": em.v("sV")}
            G["xe_guard"] = {"sS": em.v("sS"), "sT": em.v("sT"), "sV": em.v("sV")}

        def make_issuers():
            G = st_g
            wcounts, wocounts = {}, {}

            def issue_w(g):
                d_i, ptl = seq[g]
                slot = g % 3
                W("gpsimd", "sT", G["wslot_hist"][slot])
                for wt, dst in (("z", wz_blk), ("h", wh_blk)):
                    E("gpsimd", "dma_start", inc="sW", amt=16,
                      out=dst.ap().rearrange("p (b k n) -> p b k n", b=3, k=KT)[:, slot],
                      in_=Wdr[(wt, d_i)].ap().rearrange(
                          "(k p) n -> p k n", p=P)[:, :, ptl * 128:(ptl + 1) * 128])
                wcounts[g] = em.v("sW")

            def issue_wo(m):
                slot = m % 3
                W("gpsimd", "sT", G["woslot_hist"][slot])
                E("gpsimd", "dma_start", inc="sW", amt=16,
                  out=wo_blk.ap().rearrange("p (b k n) -> p b k n", b=3, k=16)[:, slot],
                  in_=WOt.ap().rearrange("(k p) n -> p k n", p=P)[:, :, m * 128:(m + 1) * 128])
                wocounts[m] = em.v("sW")

            return issue_w, issue_wo, wcounts, wocounts

        def pre_issue(bi, half):
            iw, io, wc, woc = make_issuers()
            iw(0)
            iw(1)
            io(0)
            io(1)
            st_g["pre_issued"][(bi, half)] = (iw, io, wc, woc)

        def p1_phase(bi, half, pending=None):
            G = st_g
            if half == 0:
                G["carry_cnt"] = {}
            carry_cnt = G["carry_cnt"]
            scan_cnt = {}
            last_scan_v = em.v("sV")
            p0 = G["p0_done"][bi]
            p0_sS, p0_sV = p0["sS"], p0["sV"]
            kbanks = [pt[0].ap()[:, 0:512], pt[0].ap()[:, 512:1024],
                      pk[0].ap(), pk[1].ap()]
            vbanks = [pt[1].ap()[:, 0:512], pt[1].ap()[:, 512:1024],
                      pv[0].ap(), pv[1].ap()]
            drain_pool = {i: em.v("sP") for i in range(4)}
            drain_az = {i: em.v("sS") for i in range(4)}
            fz = G.get("bank_free")
            if fz is None:
                drain_tv = {i: em.v("sV") for i in range(4)}
            elif fz["pk"] is None:
                drain_tv = {i: fz["pt"] for i in range(4)}
            else:
                pkv = max(fz["pk"], fz["pv"])
                drain_tv = {0: fz["pt"], 1: fz["pt"], 2: pkv, 3: pkv}
            pre = G["pre_issued"].pop((bi, half), None)
            if pre is None:
                issue_w, issue_wo, wcounts, wocounts = make_issuers()
                issue_w(0)
                issue_w(1)
                issue_wo(0)
                issue_wo(1)
            else:
                issue_w, issue_wo, wcounts, wocounts = pre
            for gi, (dir_i, ptile) in enumerate(seq):
                par = gi % 2
                wbuf = gi % 3
                if pending is not None and gi < KT:
                    pending[gi]()
                    if gi == KT - 1:
                        oavf = G["out_add_v"]
                        fix = max(drain_tv[2], drain_tv[3], oavf[0], oavf[1])
                        drain_tv[2] = fix
                        drain_tv[3] = fix
                        G["prev_epi"] = {"sV": em.v("sV"), "sO": em.v("sO"),
                                         "sT": em.v("sT"), "sS": em.v("sS"),
                                         "sP": em.v("sP")}
                if gi + 2 < len(seq):
                    issue_w(gi + 2)
                wcount = wcounts[gi]
                for piece in range(2):
                    rot = (gi * 2 + piece) % (2 if (pending is not None and gi < KT)
                                              else 4)
                    kb, vb = kbanks[rot], vbanks[rot]
                    if dir_i == 0:
                        t0 = half * 1024 + piece * 512
                        rhs_sl = [xn_bfT[:, k * S + t0: k * S + t0 + 512]
                                  for k in range(KT)]
                    else:
                        t0 = S - (half * 1024 + piece * 512) - 512
                        rhs_sl = [xn_bfT[:, k * S + t0: k * S + t0 + 512][:, ::-1]
                                  for k in range(KT)]
                    W("tensor", "sW", wcount)
                    W("tensor", "sS", max(drain_az[rot], p0_sS))
                    W("tensor", "sV", max(drain_tv[rot], p0_sV))
                    for k in range(KT):
                        E("tensor", "matmul", kb,
                          wz_blk.ap()[:, (wbuf * KT + k) * 128:(wbuf * KT + k + 1) * 128],
                          rhs_sl[k], start=(k == 0), stop=(k == KT - 1),
                          inc=("sT" if k == KT - 1 else None))
                    t_k = em.v("sT")
                    for k in range(KT):
                        E("tensor", "matmul", vb,
                          wh_blk.ap()[:, (wbuf * KT + k) * 128:(wbuf * KT + k + 1) * 128],
                          rhs_sl[k], start=(k == 0), stop=(k == KT - 1),
                          inc=("sT" if k == KT - 1 else None))
                    t_v = em.v("sT")
                    co = piece * 512
                    W("scalar", "sT", t_k)
                    W("scalar", "sV",
                      scan_cnt.get(seq[gi - 2], last_scan_v) if gi >= 2 else last_scan_v)
                    W("scalar", "sP", drain_pool[piece])
                    E("scalar", "activation", a_arr[par][:, co:co + 512],
                      kb, AF.Sigmoid, scale=-1.0,
                      bias=negbz[:, dir_i * KT + ptile: dir_i * KT + ptile + 1])
                    E("scalar", "activation", z_bufs[piece][:],
                      kb, AF.Sigmoid,
                      bias=biases[:, dir_i * KT + ptile: dir_i * KT + ptile + 1],
                      inc="sS")
                    W("scalar", "sT", t_v)
                    E("scalar", "activation", sv_bufs[piece][:],
                      vb, AF.Sigmoid,
                      bias=biases[:, (2 + dir_i) * KT + ptile: (2 + dir_i) * KT + ptile + 1],
                      inc="sS")
                    drain_az[rot] = em.v("sS")
                    W("vector", "sS", drain_az[rot])
                    W("vector", "sP", em.v("sP"))
                    E("vector", "scalar_tensor_tensor", t_buf[:],
                      vb,
                      bh05[:, dir_i * KT + ptile: dir_i * KT + ptile + 1],
                      sv_bufs[piece][:], ALU.add, ALU.subtract)
                    E("vector", "scalar_tensor_tensor", gh_buf[:],
                      t_buf[:], 0.0, sv_bufs[piece][:], ALU.max, ALU.add,
                      inc="sV")
                    drain_tv[rot] = em.v("sV")
                    W("gpsimd", "sV", drain_tv[rot])
                    E("gpsimd", "tensor_tensor", b_arr[par][:, co:co + 512],
                      z_bufs[piece][:], gh_buf[:], ALU.mult, inc="sP")
                    drain_pool[piece] = em.v("sP")
                G["wslot_hist"][wbuf] = em.v("sT")
                j = dir_i * KT + ptile
                W("vector", "sP", drain_pool[1])
                if half == 0:
                    W("vector", "sT", max(G["prev_epi"]["sT"], G["epi_smm_done"]))
                    E("vector", "tensor_tensor_scan",
                      h_bf[:, j * 1024:(j + 1) * 1024],
                      a_arr[par][:], b_arr[par][:], 0.0, ALU.mult, ALU.add,
                      inc="sV")
                else:
                    W("vector", "sS", carry_cnt[(dir_i, ptile)])
                    W("vector", "sT", G["epi_smm_done"])
                    E("vector", "tensor_tensor_scan",
                      h_bf[:, j * 1024:(j + 1) * 1024],
                      a_arr[par][:], b_arr[par][:], carry[:, j:j + 1],
                      ALU.mult, ALU.add, inc="sV")
                scan_cnt[(dir_i, ptile)] = em.v("sV")
                last_scan_v = em.v("sV")
                if half == 0:
                    W("scalar", "sV", last_scan_v)
                    E("scalar", "copy", carry[:, j:j + 1],
                      h_bf[:, (j + 1) * 1024 - 1:(j + 1) * 1024], inc="sS")
                    carry_cnt[(dir_i, ptile)] = em.v("sS")
            G["all_scans"][(bi, half)] = last_scan_v
            G["p1_state"] = (drain_az, drain_tv, wocounts, issue_wo)

        def epi_phase(bi, half, defer=False):
            G = st_g
            drain_az, drain_tv, wocounts, issue_wo = G["p1_state"]
            all_scans_v = G["all_scans"][(bi, half)]
            evac_s = {0: drain_az[2], 1: drain_az[3]}
            evac_f = {0: drain_tv[2], 1: drain_tv[3]}
            p_written = 0
            gg_done = em.v("sP")
            q_sS = G["q_done"][bi]
            for m in range(KT):
                wobuf = m % 3
                if m + 2 < KT:
                    issue_wo(m + 2)
                wo_here = wocounts[m]
                for piece in range(2):
                    co = piece * 512
                    W("tensor", "sV", evac_f[piece])
                    if m == 0:
                        W("tensor", "sS", q_sS)
                    t0 = half * 1024 + co
                    for k in range(KT):
                        E("tensor", "matmul", pv[piece][:],
                          qwT.ap()[:, (k * KT + m) * 128:(k * KT + m + 1) * 128],
                          q_bfT[:, k * S + t0: k * S + t0 + 512],
                          start=(k == 0), stop=(k == KT - 1),
                          inc=("sT" if k == KT - 1 else None))
                    t_f = em.v("sT")
                    W("tensor", "sW", wo_here)
                    W("tensor", "sV", all_scans_v)
                    W("tensor", "sS", evac_s[piece])
                    for j in range(16):
                        E("tensor", "matmul", pk[piece][:],
                          wo_blk.ap()[:, (wobuf * 16 + j) * 128:(wobuf * 16 + j + 1) * 128],
                          h_bf[:, j * 1024 + co: j * 1024 + co + 512],
                          start=(j == 0), stop=(j == 15),
                          inc=("sT" if j == 15 else None))
                    t_s = em.v("sT")
                    W("scalar", "sT", t_s)
                    if m >= 1:
                        W("scalar", "sV", p_written)
                    E("scalar", "activation", s_m[:, co:co + 512],
                      pk[piece][:], AF.Identity,
                      bias=biases[:, 4 * KT + m:4 * KT + m + 1], inc="sS")
                    evac_s[piece] = em.v("sS")
                    W("vector", "sT", t_f)
                    W("vector", "sP", gg_done)
                    E("vector", "tensor_scalar", ffn_m[:, co:co + 512],
                      pv[piece][:], rb[:, 1:2],
                      biases[:, 5 * KT + m:5 * KT + m + 1],
                      ALU.mult, ALU.add, inc="sV")
                    evac_f[piece] = em.v("sV")
                W("scalar", "sV", evac_f[1])
                W("scalar", "sP", gg_done)
                E("scalar", "activation", sig_m[:], ffn_m[:], AF.Sigmoid,
                  inc="sS")
                s_sig = em.v("sS")
                W("gpsimd", "sS", s_sig)
                if m == 0:
                    W("gpsimd", "sV", all_scans_v)
                E("gpsimd", "tensor_tensor", gg_m[:], ffn_m[:], sig_m[:],
                  ALU.mult, inc="sP")
                gg_done = em.v("sP")
                W("vector", "sP", gg_done)
                W("vector", "sS", evac_s[1])
                if m == 0:
                    W("vector", "sT", G.get("p_half_free", 0))
                E("vector", "tensor_tensor", p_half[:, m * 1024:(m + 1) * 1024],
                  s_m[:], gg_m[:], ALU.mult, inc="sV")
                p_written = em.v("sV")
                G["woslot_hist"][wobuf] = em.v("sT")
            G["epi_smm_done"] = em.v("sT")
            G["ep_p_written"] = p_written

            v_now = em.v("sV")
            p_now = em.v("sP")
            s_now = em.v("sS")
            oav = G["out_add_v"]; odo = G["out_dma_o"]

            def emit_out_tt(tt, xq="sync"):
                tb = tt % 2
                gd = G["xe_guard"]
                W(xq, "sV", max(oav[tb], v_now, gd["sV"]))
                W(xq, "sT", gd["sT"])
                W(xq, "sP", p_now)
                W(xq, "sS", max(s_now, gd["sS"]))
                E(xq, "dma_start", inc="sQ", amt=16,
                  out=x_epi[tb],
                  in_=X.ap()[bi, half * 1024 + tt * 128: half * 1024 + (tt + 1) * 128, :])
                q_x = em.v("sQ")
                dsts = pk if tb == 0 else pv
                W("tensor", "sV", max(p_written, oav[tb]))
                W("tensor", "sS", s_now)
                for m in range(KT):
                    E("tensor", "matmul",
                      dsts[m // 4][:, (m % 4) * 128:(m % 4 + 1) * 128],
                      p_half[:, m * 1024 + tt * 128: m * 1024 + tt * 128 + 128],
                      eye_bf[:], start=True, stop=True,
                      inc=("sT" if m == KT - 1 else None))
                t_tp = em.v("sT")
                G["p_half_free"] = t_tp
                W("vector", "sT", t_tp)
                W("vector", "sQ", q_x)
                W("vector", "sO", odo[tb])
                E("vector", "tensor_tensor", out_sb[tb][:, 0:512],
                  dsts[0].ap(), x_epi[tb][:, 0:512], ALU.add)
                E("vector", "tensor_tensor", out_sb[tb][:, 512:1024],
                  dsts[1].ap(), x_epi[tb][:, 512:1024], ALU.add, inc="sV")
                oav[tb] = em.v("sV")
                W("sync", "sV", oav[tb])
                E("sync", "dma_start", inc="sO", amt=16,
                  out=OUT.ap()[bi, half * 1024 + tt * 128: half * 1024 + (tt + 1) * 128, :],
                  in_=out_sb[tb])
                odo[tb] = em.v("sO")
                G["n_out_dma"] += 1

            if defer:
                G["bank_free"] = {"pt": p_written, "pk": None, "pv": None}
                return [(lambda t=tt: emit_out_tt(t)) for tt in range(KT)]
            for tt in range(KT):
                emit_out_tt(tt, xq="scalar")
            G["bank_free"] = {"pt": p_written, "pk": oav[0], "pv": oav[1]}
            G["prev_epi"] = {"sV": em.v("sV"), "sO": em.v("sO"), "sT": em.v("sT"),
                             "sS": em.v("sS"), "sP": em.v("sP")}
            return None

        import os
        _orch = os.environ.get("KORCH", "serial")
        if mode == "cc":
            _orch = "fused"
        if _orch in ("fused", "fused2"):
            stream_batch(0, False, False, True, acc_dummy_w)
            stream_batch(1, False, False, True, acc_dummy_w)
            if mode == "cc":
                emit_gmax_cc()
            elif dbg:
                emit_localmax(sc11[:1, 4:5])
            W("gpsimd", "sS", st_g["acc_done"].get(1, 0))
            pre_issue(0, 0)
            stream_batch(0, True, True, True, acc_dummy_s)
            p1_phase(0, 0)
            pend = epi_phase(0, 0, defer=True)
            p1_phase(0, 1, pending=pend)
            pend = epi_phase(0, 1, defer=True)
            pre_issue(1, 0)
            stream_batch(1, True, True, True, acc_dummy_s)
            p1_phase(1, 0, pending=pend)
            pend = epi_phase(1, 0, defer=True)
            p1_phase(1, 1, pending=pend)
            epi_phase(1, 1)
        elif _orch == "serial":
            pre_issue(0, 0)
            stream_batch(0, True, True, True, acc_dummy_s)
            p1_phase(0, 0)
            pend = epi_phase(0, 0, defer=True)
            p1_phase(0, 1, pending=pend)
            pend = epi_phase(0, 1, defer=True)
            pre_issue(1, 0)
            stream_batch(1, True, True, True, acc_dummy_s)
            p1_phase(1, 0, pending=pend)
            pend = epi_phase(1, 0, defer=True)
            p1_phase(1, 1, pending=pend)
            epi_phase(1, 1)
        elif _orch == "semi4":
            stream_batch(0, True, True, True, acc_dummy_s)
            p1_phase(0, 0)
            epi_phase(0, 0)
            p1_phase(0, 1)
            epi_phase(0, 1)
            stream_batch(1, False, False, True, acc_dummy_s)
            stream_batch(1, True, True, False, acc_dummy_s)
            p1_phase(1, 0)
            epi_phase(1, 0)
            p1_phase(1, 1)
            epi_phase(1, 1)
        else:
            raise ValueError(_orch)
        n_out_dma = st_g["n_out_dma"]

        W("sync", "sO", 16 * n_out_dma)
        if dbg:
            DXN = nc.dram_tensor("DXN", [P, KT * S], F32, kind="ExternalOutput")
            DQ = nc.dram_tensor("DQ", [P, KT * S], F32, kind="ExternalOutput")
            DQW = nc.dram_tensor("DQW", [P, KT * D], F32, kind="ExternalOutput")
            DRB = nc.dram_tensor("DRB", [P, 2], F32, kind="ExternalOutput")
            DGM = nc.dram_tensor("DGM", [1, 1], F32, kind="ExternalOutput")
            DH = nc.dram_tensor("DH", [P, 16 * 1024], F32, kind="ExternalOutput")
            DPH = nc.dram_tensor("DPH", [P, 8192], F32, kind="ExternalOutput")
            DSM = nc.dram_tensor("DSM", [P, 1024], F32, kind="ExternalOutput")
            DFM = nc.dram_tensor("DFM", [P, 1024], F32, kind="ExternalOutput")
            DAS = nc.dram_tensor("DAS", [P, 1], F32, kind="ExternalOutput")
            DNM = nc.dram_tensor("DNM", [P, 32], F32, kind="ExternalOutput")
            DIV = nc.dram_tensor("DIV", [P, 32], F32, kind="ExternalOutput")
            DIR = nc.dram_tensor("DIR", [P, 32], F32, kind="ExternalOutput")
            DGR = nc.dram_tensor("DGR", [1, P], F32, kind="ExternalOutput")
            DSC = nc.dram_tensor("DSC", [1, 8], F32, kind="ExternalOutput")
            for dst, srcap in ((DXN, xn_bfT.ap()), (DQ, q_bfT.ap()),
                               (DQW, qwT.ap()), (DRB, rb.ap()), (DGM, gmax[:1, :]),
                               (DH, h_bf), (DPH, p_half), (DSM, s_m), (DFM, ffn_m),
                               (DAS, asum.ap()), (DGR, grow[:1, :]), (DSC, sc11[:1, :]),
                               (DNM, negmu.ap()), (DIV, inv_a.ap()), (DIR, invr_a.ap())):
                E("gpsimd", "dma_start", inc="sG", amt=16, out=dst.ap()[:], in_=srcap)
            W("gpsimd", "sG", em.v("sG"))

        blk = st.enter_context(nc.Block())
        plan = em.plan

        @blk.sync
        def _(e):
            for f in plan["sync"]:
                f(e)

        @blk.vector
        def _(e):
            for f in plan["vector"]:
                f(e)

        @blk.scalar
        def _(e):
            for f in plan["scalar"]:
                f(e)

        @blk.tensor
        def _(e):
            for f in plan["tensor"]:
                f(e)

        @blk.gpsimd
        def _(e):
            for f in plan["gpsimd"]:
                f(e)

    return nc


_NC_CACHE = {}


def build_stats():
    nc = bass.Bass("TRN2", target_bir_lowering=False)
    X = nc.dram_tensor("X", [B_LOC, S, D], F32, kind="ExternalInput")
    OUTL = nc.dram_tensor("OUT_L", [1], F32, kind="ExternalOutput")
    st = contextlib.ExitStack()
    with st:
        sb = lambda n, sh, dt: st.enter_context(nc.sbuf_tensor(n, sh, dt))
        x_in = [sb(f"x{i}", [P, D], F32) for i in (0, 1)]
        accdum = sb("accdum", [P, D], F32)
        msum = sb("msum", [P, B_LOC * NT], F32)
        sqsum = sb("sqsum", [P, B_LOC * NT], F32)
        mean2 = sb("mean2", [P, B_LOC * NT], F32)
        varv = sb("varv", [P, B_LOC * NT], F32)
        negmu = sb("negmu", [P, B_LOC * NT], F32)
        sd_a = sb("sd_a", [P, B_LOC * NT], F32)
        inv_a = sb("inv_a", [P, B_LOC * NT], F32)
        mx_s = sb("mx_s", [P, B_LOC * NT], F32)
        mn_s = sb("mn_s", [P, B_LOC * NT], F32)
        am_s = sb("am_s", [P, B_LOC * NT], F32)
        lmax = sb("lmax", [P, 1], F32)
        grow = sb("grow", [1, P], F32)
        gmax = sb("gmax", [1, 1], F32)
        eps_b = sb("eps_b", [P, 1], F32)
        c127 = sb("c127", [1, 1], F32)
        grow_lm = sb("grow_lm", [1, 8], F32)
        em = Em(nc, st)
        E, W = em.op, em.wait
        E("vector", "memset", eps_b[:], 1e-5, inc="sV")
        vq, sq = [], []
        for bi in range(B_LOC):
            for ti in range(NT):
                gidx = bi * NT + ti
                buf = gidx % 2
                if gidx >= 2:
                    W("sync", "sV", vq[gidx - 2])
                    W("sync", "sS", sq[gidx - 2])
                E("sync", "dma_start", inc="sQ", amt=16,
                  out=x_in[buf][:], in_=X.ap()[bi, ti * 128:(ti + 1) * 128, :])
                q = em.v("sQ")
                W("scalar", "sQ", q)
                E("scalar", "activation", accdum[:], x_in[buf][:], AF.Identity,
                  accum_out=msum[:, gidx:gidx + 1])
                E("scalar", "activation", accdum[:], x_in[buf][:], AF.Square,
                  accum_out=sqsum[:, gidx:gidx + 1], inc="sS")
                sq.append(em.v("sS"))
                W("vector", "sQ", q)
                E("vector", "tensor_reduce", mx_s[:, gidx:gidx + 1], x_in[buf][:],
                  AX.X, ALU.max)
                E("vector", "tensor_reduce", mn_s[:, gidx:gidx + 1], x_in[buf][:],
                  AX.X, ALU.min, inc="sV")
                vq.append(em.v("sV"))
        W("vector", "sS", em.v("sS"))
        E("vector", "tensor_scalar_mul", negmu[:], msum[:], -1.0 / 1024.0, inc="sV")
        W("gpsimd", "sV", em.v("sV"))
        E("gpsimd", "tensor_tensor", mean2[:], negmu[:], negmu[:], ALU.mult,
          inc="sP")
        W("vector", "sP", em.v("sP"))
        E("vector", "scalar_tensor_tensor", varv[:], sqsum[:], 1.0 / 1024.0,
          mean2[:], ALU.mult, ALU.subtract, inc="sV")
        W("scalar", "sV", em.v("sV"))
        E("scalar", "activation", sd_a[:], varv[:], AF.Sqrt,
          bias=eps_b[:, 0:1], inc="sS")
        W("vector", "sS", em.v("sS"))
        E("vector", "reciprocal", inv_a[:], sd_a[:], inc="sV")
        E("vector", "tensor_tensor", am_s[:], mx_s[:], negmu[:], ALU.add)
        E("vector", "tensor_tensor", mx_s[:], mn_s[:], negmu[:], ALU.add, inc="sV")
        W("gpsimd", "sV", em.v("sV"))
        E("gpsimd", "tensor_tensor", mn_s[:], am_s[:], inv_a[:], ALU.mult)
        E("gpsimd", "tensor_tensor", am_s[:], mx_s[:], inv_a[:], ALU.mult,
          inc="sP")
        W("vector", "sP", em.v("sP"))
        E("vector", "scalar_tensor_tensor", mx_s[:], am_s[:], -1.0, mn_s[:],
          ALU.mult, ALU.max, inc="sV")
        W("scalar", "sV", em.v("sV"))
        E("scalar", "copy", sd_a[:], mx_s[:], inc="sS")
        W("vector", "sS", em.v("sS"))
        E("vector", "tensor_reduce", lmax[:, 0:1], sd_a[:], AX.X, ALU.max,
          inc="sV")
        v_lm = em.v("sV")
        W("gpsimd", "sV", v_lm)
        E("gpsimd", "dma_start", inc="sG", amt=16, out=grow[:1, :], in_=lmax[:, 0:1])
        W("vector", "sG", em.v("sG"))
        E("vector", "tensor_reduce", gmax[:1, 0:1], grow[:1, :], AX.X, ALU.max,
          inc="sV")
        v_gm = em.v("sV")
        W("sync", "sV", v_gm)
        E("sync", "dma_start", inc="sO", amt=16, out=OUTL.ap()[0:1],
          in_=gmax[:1, 0:1])
        W("sync", "sO", 16)
        blk = st.enter_context(nc.Block())
        plan = em.plan

        @blk.sync
        def _(e):
            for f in plan["sync"]:
                f(e)

        @blk.vector
        def _(e):
            for f in plan["vector"]:
                f(e)

        @blk.scalar
        def _(e):
            for f in plan["scalar"]:
                f(e)

        @blk.tensor
        def _(e):
            for f in plan["tensor"]:
                f(e)

        @blk.gpsimd
        def _(e):
            for f in plan["gpsimd"]:
                f(e)
    return nc



_PROBE_SRC = """
import contextlib
import numpy as np
import concourse.bass as bass
import concourse.mybir as mybir
from concourse.bass_utils import run_bass_kernel_spmd
F32 = mybir.dt.float32
ALU = mybir.AluOpType
nc = bass.Bass("TRN2", target_bir_lowering=False)
A = nc.dram_tensor("A", [1], F32, kind="ExternalInput")
O = nc.dram_tensor("O", [1, 1], F32, kind="ExternalOutput")
CI = nc.dram_tensor("p_ci", [1], F32)
CO = nc.dram_tensor("p_co", [1], F32)
with contextlib.ExitStack() as st:
    a = st.enter_context(nc.sbuf_tensor("a", [1, 1], F32))
    sG = st.enter_context(nc.semaphore("sG"))
    sO = st.enter_context(nc.semaphore("sO"))
    blk = st.enter_context(nc.Block())
    @blk.sync
    def _(s):
        s.wait_ge(sG, 64)
        s.dma_start(out=O[:1, :], in_=a[:1, 0:1]).then_inc(sO, 16)
        s.wait_ge(sO, 16)
    @blk.gpsimd
    def _(g):
        g.dma_start(out=a[:1, 0:1], in_=A[None, 0:1]).then_inc(sG, 16)
        g.wait_ge(sG, 16)
        g.dma_start(out=CI.ap()[0:1], in_=a[:1, 0:1]).then_inc(sG, 16)
        g.wait_ge(sG, 32)
        g.collective_compute("AllReduce", ALU.max, replica_groups=[list(range(8))],
                             ins=[CI.ap()[0:1]], outs=[CO.ap()[0:1]]).then_inc(sG, 16)
        g.wait_ge(sG, 48)
        g.dma_start(out=a[:1, 0:1], in_=CO.ap()[None, 0:1]).then_inc(sG, 16)
    @blk.vector
    def _(v):
        pass
    @blk.scalar
    def _(s):
        pass
    @blk.tensor
    def _(t):
        pass
ins = [dict(A=np.array([float(c)], np.float32)) for c in range(8)]
res = run_bass_kernel_spmd(nc, ins, core_ids=list(range(8)))
ok = all(abs(res.results[c]["O"][0, 0] - 7.0) < 1e-6 for c in range(8))
print("PROBE_OK" if ok else "PROBE_BAD")
"""

_PROBE_CACHE = {}


def _probe_collectives():
    if "ok" not in _PROBE_CACHE:
        import subprocess
        import sys
        try:
            r = subprocess.run([sys.executable, "-c", _PROBE_SRC],
                               capture_output=True, text=True, timeout=600)
            _PROBE_CACHE["ok"] = "PROBE_OK" in r.stdout
        except Exception:
            _PROBE_CACHE["ok"] = False
    return _PROBE_CACHE["ok"]


def kernel(**inputs):
    import os
    x = np.ascontiguousarray(inputs["L0_attn_x"])
    n = 8
    shards = [np.ascontiguousarray(x[i * B_LOC:(i + 1) * B_LOC]) for i in range(n)]
    base = dict(
        WZ_f=np.ascontiguousarray(inputs["L0_attn_Wz_f"]),
        WZ_b=np.ascontiguousarray(inputs["L0_attn_Wz_b"]),
        WH_f=np.ascontiguousarray(inputs["L0_attn_Wh_f"]),
        WH_b=np.ascontiguousarray(inputs["L0_attn_Wh_b"]),
        BZ_f=np.ascontiguousarray(inputs["L0_attn_bz_f"]),
        BZ_b=np.ascontiguousarray(inputs["L0_attn_bz_b"]),
        BH_f=np.ascontiguousarray(inputs["L0_attn_bh_f"]),
        BH_b=np.ascontiguousarray(inputs["L0_attn_bh_b"]),
        WO=np.ascontiguousarray(inputs["L0_attn_W_out"]),
        BO=np.ascontiguousarray(inputs["L0_attn_b_out"]),
        FW=np.ascontiguousarray(inputs["L0_ffn_weight"]),
        FB=np.ascontiguousarray(inputs["L0_ffn_bias"]),
        EYE=np.eye(P, dtype=np.float32),
    )
    kmode = os.environ.get("KMODE", "auto")
    if kmode in ("auto", "cc"):
        if kmode == "cc" or _probe_collectives():
            if "cc" not in _NC_CACHE:
                _NC_CACHE["cc"] = build_nc("cc")
            res = run_bass_kernel_spmd(_NC_CACHE["cc"],
                                       [dict(base, X=s) for s in shards],
                                       core_ids=list(range(n)))
            return np.concatenate([r["OUT"] for r in res.results], axis=0)
    if "stats" not in _NC_CACHE:
        _NC_CACHE["stats"] = build_stats()
    if "main" not in _NC_CACHE:
        _NC_CACHE["main"] = build_nc("lmax8")

    def _run(ncx, maps):
        last = None
        for _ in range(3):
            try:
                return run_bass_kernel_spmd(ncx, maps, core_ids=list(range(n)))
            except Exception as e:
                last = e
        raise last

    resA = _run(_NC_CACHE["stats"], [dict(X=s) for s in shards])
    lm8 = np.concatenate([r["OUT_L"] for r in resA.results]).astype(np.float32)
    in_maps = [dict(base, X=s, LMAX8=lm8) for s in shards]
    res = _run(_NC_CACHE["main"], in_maps)
    return np.concatenate([r["OUT"] for r in res.results], axis=0)


if __name__ == "__main__":
    nc = build_nc("lmax8")
    print("build ok; instrs:",
          sum(len(b.instructions) for f in nc.m.functions for b in f.blocks))
